# version 12
# baseline (speedup 1.0000x reference)
import numpy as np
import ml_dtypes
from contextlib import ExitStack

import concourse.bass as bass
import concourse.mybir as mybir
from concourse.bass_utils import run_bass_kernel_spmd

F32 = mybir.dt.float32
BF16 = mybir.dt.bfloat16
AF = mybir.ActivationFunctionType
ALU = mybir.AluOpType
AX = mybir.AxisListType.X

D_MODEL = 2048
D_INNER = 4096
D_LOC = 2048
D_STATE = 16
L = 4096

T_C = 512
N_CHUNK = L // T_C
G = D_LOC // 128
KT = D_MODEL // 128
N_MI = 40
MI_Z, MI_X, MI_B, MI_C = 0, 16, 20, 24
NTT = T_C // 128
T_M = T_C // 2
NTT2 = T_M // 128
CC_GROUPS = [[0, 1], [2, 3], [4, 5], [6, 7]]
LN_EPS = 1e-5
R2688 = float(np.float32(1.0) / np.float32(2688.0))
NBATCH = 4
RACE_SAFE = True
SEG_DRAINS = True
CC_FAKE = False
DEBUG_CHUNKS = None


def _tile_lhsT(w):
    M, K = w.shape
    t = w.reshape(M // 128, 128, K // 128, 128).transpose(0, 3, 2, 1)
    return np.ascontiguousarray(t.reshape(M // 128, 128, K))


def _conv_sel(conv_w):
    out = np.zeros((G, 4, 128, 128), np.float32)
    ch = np.arange(D_LOC)
    h, j = ch // 16, ch % 16
    xch = (h // 4) * 16 + j
    for g in range(G):
        for m in range(128):
            c = 128 * g + m
            out[g, :, xch[c] - 128 * (g // 4), m] = conv_w[c, 0, :]
    return out


def _b_sel():
    out = np.zeros((4, D_STATE, 128, 128), np.float32)
    for q in range(4):
        for n in range(D_STATE):
            out[q, n, 32 * q + n, 0:64] = -1.0
            out[q, n, 32 * q + 16 + n, 64:128] = -1.0
    return out.astype(ml_dtypes.bfloat16).reshape(4 * D_STATE, 128, 128)


def _c_sel():
    out = np.zeros((D_STATE, 128, 128), np.float32)
    for n in range(D_STATE):
        for h in range(8):
            out[n, 16 * h + n, 16 * h:16 * (h + 1)] = 1.0
    return out.astype(ml_dtypes.bfloat16)


def build_program():
    nc = bass.Bass("TRN2", target_bir_lowering=False,
                  detect_race_conditions=RACE_SAFE)
    st = ExitStack()
    e = st.enter_context

    hs_d = nc.dram_tensor("hs", [L, D_MODEL], F32, kind="ExternalInput").ap()
    hs2_d = nc.dram_tensor("hs2", [N_CHUNK * T_M, D_MODEL], F32, kind="ExternalInput").ap()
    wt_all = nc.dram_tensor("wt_all", [N_MI, 128, D_MODEL], F32, kind="ExternalInput").ap()
    dtin_d = nc.dram_tensor("dtin_t", [128, D_MODEL], F32, kind="ExternalInput").ap()
    dtp_d = nc.dram_tensor("dtp_t", [128, G * 128], F32, kind="ExternalInput").ap()
    dtb_d = nc.dram_tensor("dtb", [1, D_LOC], F32, kind="ExternalInput").ap()
    convsel_d = nc.dram_tensor("convsel", [G * 4, 128, 128], F32, kind="ExternalInput").ap()
    convb_d = nc.dram_tensor("convb", [1, D_LOC], F32, kind="ExternalInput").ap()
    alog_d = nc.dram_tensor("alog_t", [128, G * D_STATE], F32, kind="ExternalInput").ap()
    dvec_d = nc.dram_tensor("dvec_t", [128, G], F32, kind="ExternalInput").ap()
    wo_d = nc.dram_tensor("wo_t", [KT, 128, D_LOC], F32, kind="ExternalInput").ap()
    wffn_d = nc.dram_tensor("wffn", [2, 128, D_MODEL], F32, kind="ExternalInput").ap()
    ffnb_d = nc.dram_tensor("ffnb", [1, D_MODEL], F32, kind="ExternalInput").ap()
    xts_d = nc.dram_tensor("xts", [1, 1], F32, kind="ExternalInput").ap()
    bsel_d = nc.dram_tensor("bsel", [4 * D_STATE, 128, 128], BF16, kind="ExternalInput").ap()
    csel_d = nc.dram_tensor("csel", [D_STATE, 128, 128], BF16, kind="ExternalInput").ap()
    ones_d = nc.dram_tensor("ones_row", [128, T_C], BF16, kind="ExternalInput").ap()
    identb_d = nc.dram_tensor("identb", [128, 128], BF16, kind="ExternalInput").ap()
    identf_d = nc.dram_tensor("identf", [128, 128], F32, kind="ExternalInput").ap()

    out_d = nc.dram_tensor("out_tm", [N_CHUNK * T_M, D_MODEL], F32, kind="ExternalOutput").ap()

    wt_bf = nc.dram_tensor("wt_bf", [N_MI + 1, 128, D_MODEL], BF16).ap()
    wo_bf = nc.dram_tensor("wo_bf", [KT, 128, D_LOC], BF16).ap()
    agin_d = nc.dram_tensor("agin", [2 * KT * 128, 128], BF16).ap()
    agout_d = nc.dram_tensor("agout", [8 * 2 * KT * 128, 128], BF16).ap()
    agin4 = agin_d.rearrange("(a b p) j -> a b p j", b=KT, p=128)
    agout5 = agout_d.rearrange("(c a b p) j -> c a b p j", c=8, b=KT, p=128)
    amax_d = nc.dram_tensor("amax_d", [1, 16], F32).ap()
    amax2_d = nc.dram_tensor("amax2_d", [1, 16], F32).ap()
    cvs_bf = nc.dram_tensor("cvs_bf", [G, 128, 4 * 128], BF16).ap()
    gath_d = nc.dram_tensor("gath_d", [128, 1], F32).ap()
    sca_d = nc.dram_tensor("sca_d", [1, 8], F32).ap()
    bhi_d = nc.dram_tensor("bhi_d", [2, D_LOC], BF16).ap()
    cc_in = nc.dram_tensor("cc_in", [2, T_C, D_MODEL], F32).ap()
    cc_out = nc.dram_tensor("cc_out", [2, T_M, D_MODEL], F32).ap()

    bsel_sb = e(nc.sbuf_tensor("bsel_sb", [128, 4 * D_STATE, 128], BF16))
    csel_sb = e(nc.sbuf_tensor("csel_sb", [128, D_STATE, 128], BF16))
    dtp_sb = e(nc.sbuf_tensor("dtp_sb", [128, G, 128], BF16))
    a_scan = e(nc.sbuf_tensor("a_scan", [128, G * D_STATE], F32))
    d_sb = e(nc.sbuf_tensor("d_sb", [128, G], F32))
    dtbK = e(nc.sbuf_tensor("dtbK", [128, G, 128], BF16))
    cvbK = e(nc.sbuf_tensor("cvbK", [128, G, 128], BF16))
    ones_sb = e(nc.sbuf_tensor("ones_sb", [128, T_C], BF16))
    identb_sb = e(nc.sbuf_tensor("identb_sb", [128, 128], BF16))
    identf_sb = e(nc.sbuf_tensor("identf_sb", [128, 128], F32))
    ffnb_bc = e(nc.sbuf_tensor("ffnb_bc", [128, D_MODEL], F32))
    colbuf = e(nc.sbuf_tensor("colbuf", [128, 8], F32))
    smalls = e(nc.sbuf_tensor("smalls", [1, 16], F32))
    gath = e(nc.sbuf_tensor("gath", [1, 128], F32))
    carries = e(nc.sbuf_tensor("carries", [128, G * D_STATE], F32))
    lncols = e(nc.sbuf_tensor("lncols", [128, 12], F32))

    stagebf = e(nc.sbuf_tensor("stagebf", [128, 2048], BF16))
    cvslab = e(nc.sbuf_tensor("cvslab", [128, 1, 4 * 128], BF16))

    arena1 = e(nc.sbuf_tensor("arena1", [128, 2048], F32))
    hsn_t = e(nc.sbuf_tensor("hsn_t", [128, KT, T_C], BF16))
    x_bf = e(nc.sbuf_tensor("x_bf", [128, 4, 3 + T_C], BF16))
    b_bf = e(nc.sbuf_tensor("b_bf", [128, 8, T_C], BF16))
    dtlow_bf = e(nc.sbuf_tensor("dtlow_bf", [128, T_C], BF16))
    t2_g = e(nc.sbuf_tensor("t2_g", [128, T_C], F32))
    u_g = e(nc.sbuf_tensor("u_g", [128, T_C], F32))
    pi_g = e(nc.sbuf_tensor("pi_g", [128, T_C], F32))
    zsil_g = e(nc.sbuf_tensor("zsil_g", [128, T_C], F32))
    cbf_g = e(nc.sbuf_tensor("cbf_g", [128, T_C], BF16))
    da_b = e(nc.sbuf_tensor("da_b", [128, NBATCH, T_C], F32))
    db_b2 = e(nc.sbuf_tensor("db_b2", [128, 2, NBATCH, T_C], F32))
    s_b = e(nc.sbuf_tensor("s_b", [128, NBATCH, T_C], F32))
    sig_g = da_b[:, 0, :]

    y_g = e(nc.sbuf_tensor("y_g", [128, T_C], F32))
    y2s = e(nc.sbuf_tensor("y2s", [128, T_C], F32))
    ygate = e(nc.sbuf_tensor("ygate", [128, G, T_C], BF16))
    x2_tm = e(nc.sbuf_tensor("x2_tm", [128, NTT2, D_MODEL], F32))
    ao_sb = e(nc.sbuf_tensor("ao_sb", [128, 2048], F32))
    xdq_t = e(nc.sbuf_tensor("xdq_t", [128, KT, 128 * NTT2], BF16))
    q_a = e(nc.sbuf_tensor("q_a", [128, D_MODEL], F32))
    q_b = e(nc.sbuf_tensor("q_b", [128, D_MODEL], F32))
    q_sg = arena1
    q_st = None
    qs = e(nc.sbuf_tensor("qs", [128, 768], F32))
    bsq = e(nc.sbuf_tensor("bsq", [128, 128], F32))
    rden = e(nc.sbuf_tensor("rden", [128, 128], F32))

    wslab = e(nc.sbuf_tensor("wslab", [128, 2, D_MODEL], BF16))
    woslab = x2_tm[:].bitcast(BF16)[:, :, 0:D_LOC]
    wqslab = None

    stage32 = x2_tm[:].rearrange("p a b -> p (a b)")[:, 0:2048]
    x2n = ao_sb


    wqslab = wslab
    ps_a = e(nc.psum_tensor("ps_a", [128, 512], F32))
    ps_b = e(nc.psum_tensor("ps_b", [128, 512], F32))
    ps_c = e(nc.psum_tensor("ps_c", [128, 512], F32))
    ps_d = e(nc.psum_tensor("ps_d", [128, 512], F32))
    ps_big = e(nc.psum_tensor("ps_big", [128, 2048], F32))
    psb16 = ps_big[:].bitcast(BF16)

    sems = {}
    cnt = {}

    with nc.Block() as blk:
        for s in ["sLD", "sST", "sPE", "sACT", "sDVE", "sGP", "sC", "sWL", "sHS", "sCV", "sWS0", "sWS1", "sWO0", "sWO1", "sCI", "sCC"]:
            sems[s] = e(nc.semaphore(s))
            cnt[s] = 0

        def inc(bi, name, by=1):
            cnt[name] += by
            return bi.then_inc(sems[name], by)

        SY, PE, SC, VE, GP = blk.sync, blk.tensor, blk.scalar, blk.vector, blk.gpsimd

        def pslot(i):
            return [ps_a, ps_b, ps_c, ps_d][i][:, 0:T_C]

        def join():
            snap = dict(cnt)
            for sec, skip in [(PE, "sPE"), (SC, "sACT"), (VE, "sDVE"),
                              (GP, "sGP"), (SY, None)]:
                def _(t, skip=skip, snap=snap):
                    for s in ["sPE", "sACT", "sDVE", "sGP", "sLD", "sST", "sC",
                              "sWL", "sHS", "sCV", "sWS0", "sWS1", "sWO0",
                              "sWO1", "sCI", "sCC"]:
                        if s != skip and snap[s]:
                            t.wait_ge(sems[s], snap[s])
                sec(_)

        def _(sync):
            for dst, src in [
                (bsel_sb[:], bsel_d.rearrange("a p b -> p a b")),
                (csel_sb[:], csel_d.rearrange("a p b -> p a b")),
                (a_scan[:], alog_d), (d_sb[:], dvec_d),
                (ones_sb[:], ones_d), (identb_sb[:], identb_d), (identf_sb[:], identf_d),
                (ffnb_bc[:], ffnb_d[0:1, :].broadcast_to([128, D_MODEL])),
                (smalls[0:1, 0:1], xts_d),
                (q_a[0:1, 0:D_LOC], dtb_d),
                (q_b[0:1, 0:D_LOC], convb_d),
            ]:
                inc(sync.dma_start(dst, src), "sC", 16)
        SY(_)

        def _(vector):
            vector.wait_ge(sems["sC"], cnt["sC"])
            vector.memset(carries[:], 0.0)
            if RACE_SAFE:
                vector.drain()
            vector.memset(smalls[0:1, 1:16], 0.0)
            if RACE_SAFE:
                vector.drain()
            vector.memset(x_bf[:].rearrange("p a b -> p (a b)"), 0.0)
            if RACE_SAFE:
                vector.drain()
            vector.memset(lncols[:, 9:10], LN_EPS)
            if RACE_SAFE:
                vector.drain()
            vector.memset(dtbK[:].rearrange("p a b -> p (a b)"), 0.0)
            if RACE_SAFE:
                vector.drain()
            vector.memset(cvbK[:].rearrange("p a b -> p (a b)"), 0.0)
            if RACE_SAFE:
                vector.drain()
            for i, src in enumerate([q_a, q_b]):
                vector.tensor_copy(stagebf[0:1, 0:D_LOC], src[0:1, 0:D_LOC])
                if RACE_SAFE:
                    vector.drain()
                vector.tensor_copy(q_sg[0:1, 0:D_LOC], stagebf[0:1, 0:D_LOC])
                if RACE_SAFE:
                    vector.drain()
                vector.tensor_sub(q_sg[32:33, 0:D_LOC], src[0:1, 0:D_LOC],
                                  q_sg[0:1, 0:D_LOC])
                if RACE_SAFE:
                    vector.drain()
                vector.tensor_copy(stagebf[32 + 32 * i:33 + 32 * i, 0:D_LOC],
                                   q_sg[32:33, 0:D_LOC])
                if RACE_SAFE:
                    vector.drain()
            inc(vector.drain(), "sDVE")
        VE(_)

        def _(gpsimd):
            gpsimd.wait_ge(sems["sDVE"], cnt["sDVE"])
            inc(gpsimd.dma_start(bhi_d[0:1, :], dtb_d), "sST", 16)
            inc(gpsimd.dma_start(bhi_d[1:2, :], convb_d), "sST", 16)
            gpsimd.wait_ge(sems["sST"], cnt["sST"])
            for i, dst in enumerate([dtbK, cvbK]):
                inc(gpsimd.dma_start(
                    dst[0:1, :, :].rearrange("p a b -> p (a b)"), bhi_d[i:i + 1, :]),
                    "sST", 16)
                inc(gpsimd.dma_start(
                    dst[1:2, :, :].rearrange("p a b -> p (a b)"),
                    stagebf[32 + 32 * i:33 + 32 * i, 0:D_LOC]),
                    "sST", 16)
        GP(_)

        def _(scalar):
            scalar.wait_ge(sems["sC"], cnt["sC"])
            scalar.activation(a_scan[:], a_scan[:], AF.Exp)
            inc(scalar.drain(), "sACT")
        SC(_)

        conv_items = [(wt_all[mi], wt_bf[mi]) for mi in range(N_MI)]
        conv_items.append((dtin_d, wt_bf[N_MI]))
        conv_items.append((wo_d.rearrange("k p b -> p k b"),
                           wo_bf.rearrange("k p b -> p k b")))
        for gi in range(G):
            conv_items.append((
                convsel_d[4 * gi:4 * (gi + 1)].rearrange("k p b -> p k b"),
                cvs_bf[gi].rearrange("p (k b) -> p k b", b=128)))
        conv_items.append((dtp_d, dtp_sb[:].rearrange("p a b -> p (a b)")))
        def _(gpsimd):
            for src, dst in conv_items:
                inc(gpsimd.dma_start(dst, src), "sST", 16)
        GP(_)

        def _(vector):
            vector.wait_ge(sems["sST"], cnt["sST"])
            vector.memset(lncols[:, 0:1], 0.0)
            if RACE_SAFE:
                vector.drain()
            inc(vector.drain(), "sDVE")
        VE(_)

        for ti in range(2):
            def _(sync, ti=ti):
                sync.wait_ge(sems["sDVE"], cnt["sDVE"])
                inc(sync.dma_start(stage32[:], wffn_d[ti]), "sWL", 16)
            SY(_)

            def _(vector):
                vector.wait_ge(sems["sWL"], cnt["sWL"])
                vector.tensor_reduce(lncols[:, 1:2], stage32[:], AX, ALU.max,
                                     apply_absolute_value=True)
                if RACE_SAFE:
                    vector.drain()
                vector.tensor_max(lncols[:, 0:1], lncols[:, 0:1], lncols[:, 1:2])
                if RACE_SAFE:
                    vector.drain()
                inc(vector.drain(), "sDVE")
            VE(_)

        def _(gpsimd):
            gpsimd.wait_ge(sems["sDVE"], cnt["sDVE"])
            inc(gpsimd.dma_start(gath_d, lncols[:, 0:1]), "sST", 16)
        GP(_)

        def _(sync):
            sync.wait_ge(sems["sST"], cnt["sST"])
            inc(sync.dma_start(gath[:], gath_d.rearrange("a b -> b a")), "sWL", 16)
        SY(_)

        def _(vector):
            vector.wait_ge(sems["sWL"], cnt["sWL"])
            inc(vector.tensor_reduce(smalls[0:1, 1:2], gath[:], AX, ALU.max),
                "sDVE")
        VE(_)

        def _(gpsimd):
            gpsimd.wait_ge(sems["sDVE"], cnt["sDVE"])
            inc(gpsimd.dma_start(amax_d, smalls[0:1, 0:16]), "sST", 16)
            gpsimd.wait_ge(sems["sST"], cnt["sST"])
            inc(gpsimd.collective_compute(
                "AllReduce", ALU.max, replica_groups=[list(range(8))],
                ins=[amax_d.opt()], outs=[amax2_d.opt()]), "sCC")
        GP(_)

        def _(sync):
            sync.wait_ge(sems["sCC"], cnt["sCC"])
            inc(sync.dma_start(smalls[0:1, 1:2], amax2_d[0:1, 1:2]), "sWL", 16)
        SY(_)

        def _(vector):
            vector.wait_ge(sems["sWL"], cnt["sWL"])
            vector.tensor_scalar_mul(smalls[0:1, 2:3], smalls[0:1, 1:2], R2688)
            if RACE_SAFE:
                vector.drain()
            vector.tensor_scalar_mul(smalls[0:1, 3:4], smalls[0:1, 2:3], 6.0)
            if RACE_SAFE:
                vector.drain()
            vector.reciprocal(smalls[0:1, 4:5], smalls[0:1, 3:4])
            if RACE_SAFE:
                vector.drain()
            vector.tensor_mul(smalls[0:1, 5:6], smalls[0:1, 2:3], smalls[0:1, 0:1])
            if RACE_SAFE:
                vector.drain()
            vector.tensor_scalar_mul(smalls[0:1, 6:7], smalls[0:1, 0:1], 6.0)
            if RACE_SAFE:
                vector.drain()
            vector.reciprocal(smalls[0:1, 7:8], smalls[0:1, 6:7])
            if RACE_SAFE:
                vector.drain()
            inc(vector.drain(), "sDVE")
        VE(_)

        def _(gpsimd):
            gpsimd.wait_ge(sems["sDVE"], cnt["sDVE"])
            inc(gpsimd.dma_start(sca_d, smalls[0:1, 0:8]), "sST", 16)
        GP(_)

        def _(sync):
            sync.wait_ge(sems["sST"], cnt["sST"])
            for col, idx in [(0, 5), (1, 7), (2, 4), (3, 2), (4, 0)]:
                inc(sync.dma_start(colbuf[:, col:col + 1],
                                   sca_d[0:1, idx:idx + 1].broadcast_to([128, 1])),
                    "sWL", 16)
        SY(_)

        def emit_bs_quant(vector, v3, W, r6col, tscol):
            NB = W // 16
            c0, c1, c2, c3, c4, c5 = (qs[:, i * NB:(i + 1) * NB] for i in range(6))
            mslot = lambda j: q_b[:, j * NB:(j + 1) * NB]
            mred = q_b[:, 0:16 * NB].rearrange("p (j b) -> p b j", b=NB)
            vector.tensor_reduce(c0, v3, AX, ALU.max, apply_absolute_value=True)
            if RACE_SAFE:
                vector.drain()
            vector.scalar_tensor_tensor(c0, c0, 1.0, r6col.broadcast_to([128, NB]),
                                        ALU.mult, ALU.mult)
            if RACE_SAFE:
                vector.drain()
            vector.tensor_scalar_min(c0, c0, 448.0)
            if RACE_SAFE:
                vector.drain()
            for j, k in enumerate(range(-5, 9)):
                vector.tensor_scalar(mslot(j), c0, float(2.0 ** k),
                                     float(2.0 ** (k - 1)), ALU.is_ge, ALU.mult)
            vector.memset(mslot(14), float(2.0 ** -6))
            vector.memset(mslot(15), 0.0)
            if RACE_SAFE:
                vector.drain()
            vector.tensor_reduce(c1, mred, AX, ALU.add)
            if RACE_SAFE:
                vector.drain()
            vector.reciprocal(c2, c1)
            if RACE_SAFE:
                vector.drain()
            vector.scalar_tensor_tensor(c2, c0, 8.0, c2, ALU.mult, ALU.mult)
            if RACE_SAFE:
                vector.drain()
            mh = c2
            for j in range(16):
                vector.tensor_scalar(mslot(j), mh, float(j) + 0.5, 1.0,
                                     ALU.is_ge, ALU.mult)
            if RACE_SAFE:
                vector.drain()
            vector.tensor_reduce(c3, mred, AX, ALU.add)
            if RACE_SAFE:
                vector.drain()
            for j in range(1, 16):
                vector.tensor_scalar(mslot(j - 1), mh, float(j), 1.0,
                                     ALU.is_ge, ALU.mult)
            vector.memset(mslot(15), 0.0)
            if RACE_SAFE:
                vector.drain()
            vector.tensor_reduce(c4, mred, AX, ALU.add)
            if RACE_SAFE:
                vector.drain()
            vector.tensor_scalar_add(c5, c4, 0.5)
            if RACE_SAFE:
                vector.drain()
            vector.tensor_tensor(c5, mh, c5, ALU.is_equal)
            for j, cc in enumerate((1.0, 3.0, 5.0, 7.0, 9.0, 11.0, 13.0, 15.0)):
                vector.tensor_scalar(mslot(j), c3, cc, 1.0, ALU.is_equal, ALU.mult)
            vector.memset(q_b[:, 8 * NB:16 * NB], 0.0)
            if RACE_SAFE:
                vector.drain()
            vector.tensor_reduce(c4, mred, AX, ALU.add)
            if RACE_SAFE:
                vector.drain()
            vector.tensor_mul(c5, c5, c4)
            if RACE_SAFE:
                vector.drain()
            vector.tensor_sub(c3, c3, c5)
            if RACE_SAFE:
                vector.drain()
            vector.scalar_tensor_tensor(bsq[:, 0:NB], c3, 0.125, c1, ALU.mult, ALU.mult)
            if RACE_SAFE:
                vector.drain()
            vector.tensor_scalar(c0, bsq[:, 0:NB], 0.0, None, ALU.is_equal)
            if RACE_SAFE:
                vector.drain()
            vector.tensor_add(c0, c0, bsq[:, 0:NB])
            if RACE_SAFE:
                vector.drain()
            vector.scalar_tensor_tensor(c0, c0, 1.0, tscol.broadcast_to([128, NB]),
                                        ALU.mult, ALU.mult)
            if RACE_SAFE:
                vector.drain()
            vector.reciprocal(rden[:, 0:NB], c0)
            if RACE_SAFE:
                vector.drain()

        def emit_xq(vector, v_ap, W, out_bf):
            NB = W // 16
            rb = rden[:, 0:NB].unsqueeze(2).to_broadcast([128, NB, 16])
            vector.tensor_scalar_mul(q_sg[:, 0:W], v_ap, -1.0)
            if RACE_SAFE:
                vector.drain()
            vector.tensor_max(q_sg[:, 0:W], q_sg[:, 0:W], v_ap)
            if RACE_SAFE:
                vector.drain()
            vector.tensor_tensor(q_a[:, 0:W].rearrange("p (a b) -> p a b", b=16),
                                 q_sg[:, 0:W].rearrange("p (a b) -> p a b", b=16),
                                 rb, ALU.mult)
            if RACE_SAFE:
                vector.drain()
            a = q_a[:, 0:W]
            acc = q_b[:, 0:W]
            vector.tensor_scalar(acc, a, 0.25, 1.0, ALU.is_gt, ALU.mult)
            if RACE_SAFE:
                vector.drain()
            for thr, op, rep in [(0.75, ALU.is_ge, 1), (1.25, ALU.is_gt, 1),
                                 (1.75, ALU.is_ge, 1), (2.5, ALU.is_gt, 2),
                                 (3.5, ALU.is_ge, 2), (5.0, ALU.is_gt, 4)]:
                for _r in range(rep):
                    vector.scalar_tensor_tensor(acc, a, thr, acc, op, ALU.add)
                    if RACE_SAFE:
                        vector.drain()
            vector.tensor_scalar(q_sg[:, 0:W], v_ap, 0.0, -2.0, ALU.is_lt, ALU.mult)
            if RACE_SAFE:
                vector.drain()
            vector.tensor_scalar_add(q_sg[:, 0:W], q_sg[:, 0:W], 1.0)
            if RACE_SAFE:
                vector.drain()
            bb = bsq[:, 0:NB].unsqueeze(2).to_broadcast([128, NB, 16])
            vector.scalar_tensor_tensor(q_sg[:, 0:W].rearrange("p (a b) -> p a b", b=16),
                                        q_sg[:, 0:W].rearrange("p (a b) -> p a b", b=16),
                                        0.5, bb, ALU.mult, ALU.mult)
            if RACE_SAFE:
                vector.drain()
            vector.tensor_mul(out_bf, acc, q_sg[:, 0:W])
            if RACE_SAFE:
                vector.drain()

        for ti in range(2):
            def _(sync, ti=ti):
                sync.wait_ge(sems["sST"], cnt["sST"])
                sync.wait_ge(sems["sDVE"], cnt["sDVE"])
                inc(sync.dma_start(stage32[:], wffn_d[ti]), "sWL", 16)
            SY(_)

            def _(vector, ti=ti):
                vector.wait_ge(sems["sWL"], cnt["sWL"])
                vector.wait_ge(sems["sACT"], cnt["sACT"])
                emit_bs_quant(vector, stage32.rearrange("p (a b) -> p a b", b=16),
                              2048, colbuf[:, 2:3], colbuf[:, 3:4])
                emit_xq(vector, stage32, 2048, stagebf[:])
                inc(vector.drain(), "sDVE")
            VE(_)

            def _(tensor, ti=ti):
                tensor.wait_ge(sems["sDVE"], cnt["sDVE"])
                tensor.wait_ge(sems["sACT"], cnt["sACT"])
                bi = None
                for kd in range(KT):
                    bi = tensor.transpose(psb16[:, 128 * kd:128 * (kd + 1)],
                                          stagebf[:, 128 * kd:128 * (kd + 1)],
                                          identb_sb[:])
                inc(bi, "sPE")
            PE(_)

            def _(scalar, ti=ti):
                scalar.wait_ge(sems["sPE"], cnt["sPE"])
                scalar.wait_ge(sems["sDVE"], cnt["sDVE"])
                inc(scalar.activation(stagebf[:], psb16[:, 0:2048], AF.Copy), "sACT")
            SC(_)

            def _(gpsimd, ti=ti):
                gpsimd.wait_ge(sems["sACT"], cnt["sACT"])
                for kd in range(KT):
                    inc(gpsimd.dma_start(
                        agin4[ti, kd],
                        stagebf[:, 128 * kd:128 * (kd + 1)]), "sST", 16)
            GP(_)

        def _(gpsimd):
            gpsimd.wait_ge(sems["sST"], cnt["sST"])
            inc(gpsimd.collective_compute(
                "AllGather", ALU.bypass, replica_groups=[list(range(8))],
                ins=[agin_d.opt()], outs=[agout_d.opt()]), "sCC")
        GP(_)

        wpar = [0]
        wfree = {0: 0, 1: 0}
        wopar = [0]
        wofree = {0: 0, 1: 0}
        wqpar = [0]
        wqfree = {0: 0, 1: 0}
        cvpar = [0]
        cvfree = {0: 0, 1: 0}

        def load_wslab(mi):
            p = wpar[0]
            wpar[0] ^= 1
            sname = "sWS%d" % p

            def _(sync, mi=mi, p=p, need=wfree[p], sname=sname):
                if need:
                    sync.wait_ge(sems["sPE"], need)
                inc(sync.dma_start(wslab[:, p], wt_bf[mi]), sname, 16)
            SY(_)
            return p, (sname, cnt[sname])

        def mm_group(psum_ap, lhsT_list, rhs_list, extra_waits=()):
            def _(tensor, psum_ap=psum_ap, lhsT_list=list(lhsT_list),
                  rhs_list=list(rhs_list), extra=list(extra_waits)):
                for s, v in extra:
                    if v:
                        tensor.wait_ge(sems[s], v)
                n = len(lhsT_list)
                bi = None
                for i in range(n):
                    bi = tensor.matmul(psum_ap, lhsT_list[i], rhs_list[i],
                                       start=(i == 0), stop=(i == n - 1))
                inc(bi, "sPE")
            PE(_)
            return cnt["sPE"]

        for ci in range(DEBUG_CHUNKS or N_CHUNK):
            t0 = ci * T_C
            join()

            for ti in range(NTT):
                def _(sync, ti=ti):
                    if ti == 0:
                        pass
                    else:
                        sync.wait_ge(sems["sACT"], cnt["sACT"])
                    inc(sync.dma_start(
                        arena1[:], hs_d[t0 + 128 * ti:t0 + 128 * (ti + 1), :]),
                        "sHS", 16)
                SY(_)

                def _(scalar, ti=ti):
                    scalar.wait_ge(sems["sHS"], cnt["sHS"])
                    scalar.wait_ge(sems["sDVE"], cnt["sDVE"])
                    bi = scalar.activation(ao_sb[:, 0:2048], arena1[:], AF.Identity,
                                           accum_out=lncols[:, 0:1])
                    bi = scalar.activation(q_b[:, 0:2048], arena1[:], AF.Square,
                                           accum_out=lncols[:, 1:2])
                    inc(bi, "sACT")
                SC(_)

                def _(vector, ti=ti):
                    vector.wait_ge(sems["sACT"], cnt["sACT"])
                    vector.tensor_scalar_mul(lncols[:, 2:3], lncols[:, 0:1],
                                             1.0 / D_MODEL)
                    vector.tensor_scalar_mul(lncols[:, 3:4], lncols[:, 1:2],
                                             1.0 / D_MODEL)
                    if RACE_SAFE:
                        vector.drain()
                    vector.tensor_mul(lncols[:, 4:5], lncols[:, 2:3], lncols[:, 2:3])
                    if RACE_SAFE:
                        vector.drain()
                    inc(vector.tensor_sub(lncols[:, 5:6], lncols[:, 3:4],
                                          lncols[:, 4:5]), "sDVE")
                VE(_)

                def _(scalar, ti=ti):
                    scalar.wait_ge(sems["sDVE"], cnt["sDVE"])
                    scalar.activation(lncols[:, 8:9], lncols[:, 5:6], AF.Ln,
                                      bias=lncols[:, 9:10])
                    scalar.drain()
                    inc(scalar.activation(lncols[:, 8:9], lncols[:, 8:9], AF.Exp,
                                          scale=0.5), "sACT")
                SC(_)

                def _(vector, ti=ti):
                    vector.wait_ge(sems["sACT"], cnt["sACT"])
                    vector.reciprocal(lncols[:, 6:7], lncols[:, 8:9])
                    if RACE_SAFE:
                        vector.drain()
                    inc(vector.scalar_tensor_tensor(lncols[:, 7:8], lncols[:, 2:3],
                                                    -1.0, lncols[:, 6:7], ALU.mult,
                                                    ALU.mult), "sDVE")
                VE(_)

                def _(scalar, ti=ti):
                    scalar.wait_ge(sems["sDVE"], cnt["sDVE"])
                    inc(scalar.activation(stagebf[:], arena1[:], AF.Identity,
                                          scale=lncols[:, 6:7], bias=lncols[:, 7:8]),
                        "sACT")
                SC(_)

                def _(tensor, ti=ti):
                    tensor.wait_ge(sems["sACT"], cnt["sACT"])
                    bi = None
                    for kt in range(KT):
                        bi = tensor.transpose(psb16[:, 128 * kt:128 * (kt + 1)],
                                              stagebf[:, 128 * kt:128 * (kt + 1)],
                                              identb_sb[:])
                    inc(bi, "sPE")
                PE(_)

                def _(scalar, ti=ti):
                    scalar.wait_ge(sems["sPE"], cnt["sPE"])
                    inc(scalar.activation(
                        hsn_t[:, :, 128 * ti:128 * (ti + 1)],
                        psb16[:, 0:2048].rearrange("p (a b) -> p a b", b=128),
                        AF.Copy), "sACT")
                SC(_)

            if ci > 0:
                def _(vector):
                    for j in range(4):
                        vector.tensor_copy(x_bf[:, j, 0:3], x_bf[:, j, T_C:T_C + 3])
                        if RACE_SAFE:
                            vector.drain()
                    inc(vector.drain(), "sDVE")
                VE(_)
            xtail_dve = cnt["sDVE"]

            psx = [ps_a, ps_b, ps_c, ps_d]
            for j, mi in enumerate(list(range(MI_X, MI_C)) + [-1]):
                bank = psx[j % 4][:, 0:T_C]
                p, ldv = load_wslab(mi if mi >= 0 else N_MI)
                mm_group(bank,
                         [wslab[:, p, 128 * kt:128 * (kt + 1)] for kt in range(KT)],
                         [hsn_t[:, kt, :] for kt in range(KT)],
                         [ldv, ("sACT", cnt["sACT"]), ("sDVE", cnt["sDVE"])])
                wfree[p] = cnt["sPE"]

                def _(scalar, mi=mi, bank=bank, xtail_dve=xtail_dve):
                    scalar.wait_ge(sems["sPE"], cnt["sPE"])
                    if 0 <= mi < MI_B:
                        scalar.wait_ge(sems["sDVE"], xtail_dve)
                        dst = x_bf[:, mi - MI_X, 3:3 + T_C]
                    elif mi >= 0:
                        dst = b_bf[:, mi - MI_B, :]
                    else:
                        dst = dtlow_bf[:]
                    inc(scalar.activation(dst, bank, AF.Copy), "sACT")
                SC(_)

            slotfree = [0]
            gp_tree = {}
            for g in range(G):
                q4 = g % 4
                xt_j = g // 4
                cp = 0

                def _(sync, g=g, cp=cp, need=cvfree[cp]):
                    if need:
                        sync.wait_ge(sems["sPE"], need)
                    inc(sync.dma_start(cvslab[:, cp], cvs_bf[g]), "sCV", 16)
                SY(_)
                cv_ld = cnt["sCV"]

                mm_group(ps_a[:, 0:T_C],
                         [dtp_sb[:, g, :], dtbK[:, g, :]],
                         [dtlow_bf[:], ones_sb[:]],
                         [("sACT", cnt["sACT"]), ("sDVE", cnt["sDVE"])])

                def _(scalar, g=g, dvew=cnt["sDVE"]):
                    scalar.wait_ge(sems["sPE"], cnt["sPE"])
                    scalar.wait_ge(sems["sDVE"], dvew)
                    inc(scalar.activation(sig_g, ps_a[:, 0:T_C], AF.Sigmoid,
                                          scale=-1.0), "sACT")
                SC(_)

                mm_group(ps_b[:, 0:T_C],
                         [cvslab[:, cp, 128 * k:128 * (k + 1)] for k in range(4)] +
                         [cvbK[:, g, :]],
                         [x_bf[:, xt_j, k:k + T_C] for k in range(4)] + [ones_sb[:]],
                         [("sCV", cv_ld), ("sACT", cnt["sACT"]),
                          ("sDVE", cnt["sDVE"])])
                cvfree[cp] = cnt["sPE"]
                pe_conv = cnt["sPE"]

                p, ldv = load_wslab(MI_Z + g)
                mm_group(ps_c[:, 0:T_C],
                         [wslab[:, p, 128 * kt:128 * (kt + 1)] for kt in range(KT)],
                         [hsn_t[:, kt, :] for kt in range(KT)],
                         [ldv, ("sACT", cnt["sACT"]), ("sDVE", cnt["sDVE"])])
                wfree[p] = cnt["sPE"]
                pe_z = cnt["sPE"]

                p, ldv = load_wslab(MI_C + g)
                mm_group(ps_d[:, 0:T_C],
                         [wslab[:, p, 128 * kt:128 * (kt + 1)] for kt in range(KT)],
                         [hsn_t[:, kt, :] for kt in range(KT)],
                         [ldv, ("sACT", cnt["sACT"]), ("sDVE", cnt["sDVE"])])
                wfree[p] = cnt["sPE"]
                pe_c = cnt["sPE"]

                def _(scalar, g=g, pe_conv=pe_conv, pe_z=pe_z, pe_c=pe_c,
                      dvw=cnt["sDVE"]):
                    scalar.drain()
                    scalar.activation(t2_g[:], sig_g, AF.Ln)
                    scalar.drain()
                    scalar.wait_ge(sems["sPE"], pe_z)
                    if dvw:
                        scalar.wait_ge(sems["sDVE"], dvw)
                    scalar.activation(u_g[:], ps_b[:, 0:T_C], AF.Silu)
                    inc(scalar.activation(zsil_g[:], ps_c[:, 0:T_C], AF.Silu), "sACT")
                SC(_)
                act_uz = cnt["sACT"]

                def _(scalar, g=g, pe_c=pe_c):
                    scalar.wait_ge(sems["sPE"], pe_c)
                    scalar.activation(cbf_g[:], ps_d[:, 0:T_C], AF.Copy)
                    bi = None
                    for k in range(NBATCH):
                        bi = scalar.activation(
                            da_b[:, k, :], t2_g[:], AF.Exp,
                            scale=a_scan[:, 16 * g + 0 + k:16 * g + 0 + k + 1])
                    inc(bi, "sACT")
                SC(_)
                act_da = {0: cnt["sACT"]}

                def _(vector, g=g, act_uz=act_uz):
                    vector.wait_ge(sems["sACT"], act_uz)
                    inc(vector.tensor_mul(pi_g[:], t2_g[:], u_g[:]), "sDVE")
                VE(_)
                dve_pi = cnt["sDVE"]

                nbat = D_STATE // NBATCH
                seg_inc = {}
                for nb in range(nbat):
                    n0 = nb * NBATCH
                    sl0 = 0
                    pp = nb % 2
                    db_b = db_b2[:, pp]

                    def _(tensor, g=g, n0=n0, q4=q4, xt_j=xt_j, sl0=sl0,
                          free=slotfree[0], da=act_da[nb]):
                        if free:
                            tensor.wait_ge(sems["sDVE"], free)
                        tensor.wait_ge(sems["sACT"], da)
                        bi = None
                        for k in range(NBATCH):
                            bi = tensor.matmul(
                                ps_big[:, (sl0 + k) * T_C:(sl0 + k + 1) * T_C],
                                bsel_sb[:, 16 * q4 + n0 + k, :],
                                b_bf[:, xt_j, :], start=True, stop=True)
                        inc(bi, "sPE")
                    PE(_)
                    pe_bb = cnt["sPE"]

                    def _(vector, db_b=db_b, sl0=sl0, pe_bb=pe_bb,
                          dvew=cnt["sDVE"], gpt=gp_tree.get(pp, 0)):
                        vector.wait_ge(sems["sPE"], pe_bb)
                        vector.wait_ge(sems["sDVE"], dvew)
                        if gpt:
                            vector.wait_ge(sems["sGP"], gpt)
                        bi = None
                        for k in range(NBATCH):
                            bi = vector.tensor_mul(
                                db_b[:, k, :], pi_g[:],
                                ps_big[:, (sl0 + k) * T_C:(sl0 + k + 1) * T_C])
                        inc(bi, "sDVE")
                    VE(_)
                    dve_db = cnt["sDVE"]

                    def _(tensor, n0=n0, sl0=sl0, dve_db=dve_db, pe_c=pe_c):
                        tensor.wait_ge(sems["sDVE"], dve_db)
                        bi = None
                        for k in range(NBATCH):
                            bi = tensor.matmul(
                                ps_big[:, (sl0 + k) * T_C:(sl0 + k + 1) * T_C],
                                csel_sb[:, n0 + k, :],
                                cbf_g[:], start=True, stop=True)
                        inc(bi, "sPE")
                    PE(_)
                    pe_cb = cnt["sPE"]

                    def _(vector, g=g, n0=n0, db_b=db_b):
                        if SEG_DRAINS:
                            vector.drain()
                        bi = None
                        for k in range(NBATCH):
                            n = n0 + k
                            bi = vector.tensor_tensor_scan(
                                s_b[:, k, :], da_b[:, k, :], db_b[:, k, :],
                                carries[:, 16 * g + n:16 * g + n + 1],
                                ALU.mult, ALU.add)
                        inc(bi, "sDVE")
                    VE(_)
                    dve_scan = cnt["sDVE"]

                    if nb + 1 < nbat:
                        def _(scalar, g=g, nb=nb, dve_scan=dve_scan):
                            scalar.wait_ge(sems["sDVE"], dve_scan)
                            bi = None
                            for k in range(NBATCH):
                                n = (nb + 1) * NBATCH + k
                                bi = scalar.activation(
                                    da_b[:, k, :], t2_g[:], AF.Exp,
                                    scale=a_scan[:, 16 * g + n:16 * g + n + 1])
                            inc(bi, "sACT")
                        SC(_)
                        act_da[nb + 1] = cnt["sACT"]

                    def _(vector, g=g, n0=n0, sl0=sl0, pe_cb=pe_cb, db_b=db_b):
                        if SEG_DRAINS:
                            vector.drain()
                        vector.tensor_copy(
                            carries[:, 16 * g + n0:16 * g + n0 + NBATCH],
                            s_b[:, :, T_C - 1:T_C].squeeze(2))
                        vector.wait_ge(sems["sPE"], pe_cb)
                        bi = None
                        for k in range(NBATCH):
                            bi = vector.tensor_mul(
                                db_b[:, k, :], s_b[:, k, :],
                                ps_big[:, (sl0 + k) * T_C:(sl0 + k + 1) * T_C])
                        inc(bi, "sDVE")
                    VE(_)
                    slotfree[0] = cnt["sDVE"]
                    seg_inc[nb] = cnt["sDVE"]

                    def _(gpsimd, nb=nb, db_b=db_b, v=cnt["sDVE"]):
                        gpsimd.wait_ge(sems["sDVE"], v)
                        gpsimd.drain()
                        tgt = y_g if nb == 0 else y2s
                        gpsimd.tensor_add(tgt[:], db_b[:, 0, :], db_b[:, 1, :])
                        gpsimd.drain()
                        gpsimd.tensor_add(tgt[:], tgt[:], db_b[:, 2, :])
                        gpsimd.drain()
                        bi = gpsimd.tensor_add(tgt[:], tgt[:], db_b[:, 3, :])
                        if nb > 0:
                            gpsimd.drain()
                            bi = gpsimd.tensor_add(y_g[:], y_g[:], y2s[:])
                        inc(bi, "sGP")
                    GP(_)
                    gp_tree[pp] = cnt["sGP"]

                def _(vector, g=g, act_uz=act_uz, gpw=cnt["sGP"]):
                    vector.wait_ge(sems["sACT"], act_uz)
                    vector.wait_ge(sems["sGP"], gpw)
                    vector.scalar_tensor_tensor(y_g[:], u_g[:], d_sb[:, g:g + 1],
                                                y_g[:], ALU.mult, ALU.add)
                    if RACE_SAFE:
                        vector.drain()
                    inc(vector.tensor_mul(ygate[:, g, :], y_g[:], zsil_g[:]),
                        "sDVE")
                VE(_)
                gate_dve = cnt["sDVE"]

            join()
            par = ci % 2

            def wvbank(w, j):
                if w % 2 == 0:
                    return ps_big[:, 512 * j:512 * (j + 1)]
                return [ps_a, ps_b, ps_c, ps_d][j][:, 0:T_C]

            def wvtile(w, tl, j):
                if w % 2 == 0:
                    return ps_big[:, (tl * 4 + j) * 128:(tl * 4 + j + 1) * 128]
                return [ps_a, ps_b, ps_c, ps_d][tl][:, 128 * j:128 * (j + 1)]

            arena_evac = {}
            for w in range(4):
                for j in range(4):
                    m = 4 * w + j
                    pq = wopar[0]
                    wopar[0] ^= 1
                    wos = "sWO%d" % pq

                    def _(sync, m=m, pq=pq, need=wofree[pq], wos=wos):
                        if need:
                            sync.wait_ge(sems["sPE"], need)
                        inc(sync.dma_start(woslab[:, pq], wo_bf[m]), wos, 16)
                    SY(_)
                    wo_ld = cnt[wos]

                    def _(tensor, w=w, j=j, pq=pq, wo_ld=wo_ld, wos=wos,
                          psfree=arena_evac.get(w - 2, 0)):
                        tensor.wait_ge(sems[wos], wo_ld)
                        if psfree:
                            tensor.wait_ge(sems["sACT"], psfree)
                        bi = None
                        for kg in range(G):
                            bi = tensor.matmul(
                                wvbank(w, j), woslab[:, pq, 128 * kg:128 * (kg + 1)],
                                ygate[:, kg, :], start=(kg == 0), stop=(kg == G - 1))
                        inc(bi, "sPE")
                    PE(_)
                    wofree[pq] = cnt["sPE"]

                def _(scalar, w=w, pe=cnt["sPE"]):
                    scalar.wait_ge(sems["sPE"], pe)
                    for j in range(4):
                        scalar.activation(ao_sb[:, 512 * j:512 * (j + 1)],
                                          wvbank(w, j), AF.Copy)
                    inc(scalar.drain(), "sACT")
                SC(_)
                act_evac = cnt["sACT"]

                def _(tensor, w=w, act_evac=act_evac):
                    tensor.wait_ge(sems["sACT"], act_evac)
                    bi = None
                    for tl in range(NTT):
                        for j in range(4):
                            bi = tensor.transpose(
                                wvtile(w, tl, j),
                                ao_sb[:, 512 * j + 128 * tl:512 * j + 128 * (tl + 1)],
                                identf_sb[:])
                    inc(bi, "sPE")
                PE(_)
                pe_tr = cnt["sPE"]

                def _(scalar, w=w, pe_tr=pe_tr, cis=cnt["sCI"]):
                    scalar.wait_ge(sems["sPE"], pe_tr)
                    if cis:
                        scalar.wait_ge(sems["sCI"], cis)
                    if w % 2 == 0:
                        scalar.activation(arena1[:], ps_big[:, 0:2048], AF.Copy)
                    else:
                        for tl in range(NTT):
                            scalar.activation(arena1[:, 512 * tl:512 * (tl + 1)],
                                              [ps_a, ps_b, ps_c, ps_d][tl][:, 0:512],
                                              AF.Copy)
                    inc(scalar.drain(), "sACT")
                SC(_)
                arena_evac[w] = cnt["sACT"]

                def _(gpsimd, w=w, par=par, ae=cnt["sACT"]):
                    gpsimd.wait_ge(sems["sACT"], ae)
                    for tl in range(NTT):
                        inc(gpsimd.dma_start(
                            cc_in[par][128 * tl:128 * (tl + 1),
                                       512 * w:512 * (w + 1)],
                            arena1[:, 512 * tl:512 * (tl + 1)]), "sCI", 16)
                GP(_)

            def _(gpsimd, par=par, cis=cnt["sCI"]):
                gpsimd.wait_ge(sems["sCI"], cis)
                if CC_FAKE:
                    inc(gpsimd.dma_start(cc_out[par], cc_in[par][0:T_M, :]),
                        "sCC", 16)
                else:
                    inc(gpsimd.collective_compute(
                        "ReduceScatter", ALU.add, replica_groups=CC_GROUPS,
                        ins=[cc_in[par].opt()], outs=[cc_out[par].opt()]), "sCC")
            GP(_)

            join()
            for tl in range(NTT2):
                def _(sync, tl=tl, ci=ci, par=par, ccn=cnt["sCC"],
                      dvew=cnt["sDVE"]):
                    sync.wait_ge(sems["sCC"], ccn)
                    if tl:
                        sync.wait_ge(sems["sDVE"], dvew)
                    inc(sync.dma_start(arena1[:],
                                       cc_out[par][128 * tl:128 * (tl + 1), :]),
                        "sHS", 16)
                    inc(sync.dma_start(
                        q_b[:, 0:2048],
                        hs2_d[ci * T_M + 128 * tl:ci * T_M + 128 * (tl + 1), :]),
                        "sHS", 16)
                SY(_)

                def _(vector, tl=tl, hsl=cnt["sHS"]):
                    vector.wait_ge(sems["sHS"], hsl)
                    vector.tensor_add(x2_tm[:, tl, :], arena1[:], q_b[:, 0:2048])
                    if RACE_SAFE:
                        vector.drain()
                    inc(vector.drain(), "sDVE")
                VE(_)

            for tl in range(NTT2):
                def _(scalar, tl=tl):
                    scalar.wait_ge(sems["sDVE"], cnt["sDVE"])
                    scalar.wait_ge(sems["sPE"], cnt["sPE"])
                    xa = x2_tm[:, tl, :]
                    bi = scalar.activation(ao_sb[:, 0:2048], xa, AF.Identity,
                                           accum_out=lncols[:, 0:1])
                    bi = scalar.activation(q_b[:, 0:2048], xa, AF.Square,
                                           accum_out=lncols[:, 1:2])
                    inc(bi, "sACT")
                SC(_)

                def _(vector, tl=tl):
                    vector.wait_ge(sems["sACT"], cnt["sACT"])
                    vector.tensor_scalar_mul(lncols[:, 2:3], lncols[:, 0:1],
                                             1.0 / D_MODEL)
                    vector.tensor_scalar_mul(lncols[:, 3:4], lncols[:, 1:2],
                                             1.0 / D_MODEL)
                    if RACE_SAFE:
                        vector.drain()
                    vector.tensor_mul(lncols[:, 4:5], lncols[:, 2:3], lncols[:, 2:3])
                    if RACE_SAFE:
                        vector.drain()
                    inc(vector.tensor_sub(lncols[:, 5:6], lncols[:, 3:4],
                                          lncols[:, 4:5]), "sDVE")
                VE(_)

                def _(scalar, tl=tl):
                    scalar.wait_ge(sems["sDVE"], cnt["sDVE"])
                    scalar.activation(lncols[:, 8:9], lncols[:, 5:6], AF.Ln,
                                      bias=lncols[:, 9:10])
                    scalar.drain()
                    inc(scalar.activation(lncols[:, 8:9], lncols[:, 8:9], AF.Exp,
                                          scale=0.5), "sACT")
                SC(_)

                def _(vector, tl=tl):
                    vector.wait_ge(sems["sACT"], cnt["sACT"])
                    vector.reciprocal(lncols[:, 6:7], lncols[:, 8:9])
                    if RACE_SAFE:
                        vector.drain()
                    inc(vector.scalar_tensor_tensor(lncols[:, 7:8], lncols[:, 2:3],
                                                    -1.0, lncols[:, 6:7], ALU.mult,
                                                    ALU.mult), "sDVE")
                VE(_)

                def _(scalar, tl=tl):
                    scalar.wait_ge(sems["sDVE"], cnt["sDVE"])
                    inc(scalar.activation(x2n[:, 0:2048], x2_tm[:, tl, :], AF.Identity,
                                          scale=lncols[:, 6:7], bias=lncols[:, 7:8]),
                        "sACT")
                SC(_)

                def _(vector, tl=tl):
                    vector.wait_ge(sems["sACT"], cnt["sACT"])
                    emit_bs_quant(vector, x2n[:, 0:2048].rearrange("p (a b) -> p a b", b=16),
                                  2048, colbuf[:, 1:2], colbuf[:, 4:5])
                    emit_xq(vector, x2n[:, 0:2048], 2048, stagebf[:])
                    inc(vector.drain(), "sDVE")
                VE(_)

                def _(tensor, tl=tl):
                    tensor.wait_ge(sems["sDVE"], cnt["sDVE"])
                    tensor.wait_ge(sems["sACT"], cnt["sACT"])
                    bi = None
                    for kd in range(KT):
                        bi = tensor.transpose(psb16[:, 128 * kd:128 * (kd + 1)],
                                              stagebf[:, 128 * kd:128 * (kd + 1)],
                                              identb_sb[:])
                    inc(bi, "sPE")
                PE(_)

                def _(scalar, tl=tl):
                    scalar.wait_ge(sems["sPE"], cnt["sPE"])
                    inc(scalar.activation(
                        xdq_t[:, :, 128 * tl:128 * (tl + 1)],
                        psb16[:, 0:2048].rearrange("p (a b) -> p a b", b=128),
                        AF.Copy), "sACT")
                SC(_)

            join()

            def fslot(i):
                if i < 4:
                    return ps_big[:, 512 * i:512 * (i + 1)]
                return [ps_a, ps_b, ps_c, ps_d][i - 4][:, :]

            for kd in range(KT):
                pq = wqpar[0]
                wqpar[0] ^= 1

                wqs = "sWS%d" % pq

                def _(sync, kd=kd, pq=pq, need=wqfree[pq], wqs=wqs):
                    if need:
                        sync.wait_ge(sems["sPE"], need)
                    inc(sync.dma_start(
                        wqslab[:, pq, :].rearrange("p (c a j) -> p c a j",
                                                   c=8, a=2),
                        agout5.rearrange("c a b p j -> b p c a j")[kd]),
                        wqs, 16)
                SY(_)
                wq_ld = cnt[wqs]

                def _(tensor, kd=kd, pq=pq, wq_ld=wq_ld, wqs=wqs):
                    tensor.wait_ge(sems[wqs], wq_ld)
                    bi = None
                    for tl in range(NTT2):
                        for ncol in range(4):
                            bi = tensor.matmul(
                                fslot(tl * 4 + ncol),
                                xdq_t[:, kd, 128 * tl:128 * (tl + 1)],
                                wqslab[:, pq, 512 * ncol:512 * (ncol + 1)],
                                start=(kd == 0), stop=(kd == KT - 1))
                    inc(bi, "sPE")
                PE(_)
                wqfree[pq] = cnt["sPE"]
            pe_ffn = cnt["sPE"]

            for tl in range(NTT2):
                for ncol in range(4):
                    def _(vector, tl=tl, ncol=ncol, pe_ffn=pe_ffn):
                        vector.wait_ge(sems["sPE"], pe_ffn)
                        vector.wait_ge(sems["sST"], cnt["sST"])
                        vector.scalar_tensor_tensor(
                            q_a[:, 0:512], fslot(tl * 4 + ncol), colbuf[:, 0:1],
                            ffnb_bc[:, 512 * ncol:512 * (ncol + 1)],
                            ALU.mult, ALU.add)
                        if RACE_SAFE:
                            vector.drain()
                        vector.tensor_add(
                            q_a[:, 512:1024], q_a[:, 0:512],
                            x2_tm[:, tl, 512 * ncol:512 * (ncol + 1)])
                        if RACE_SAFE:
                            vector.drain()
                        inc(vector.drain(), "sDVE")
                    VE(_)

                    def _(gpsimd, tl=tl, ncol=ncol, ci=ci):
                        gpsimd.wait_ge(sems["sDVE"], cnt["sDVE"])
                        inc(gpsimd.dma_start(
                            out_d[ci * T_M + 128 * tl:ci * T_M + 128 * (tl + 1),
                                  512 * ncol:512 * (ncol + 1)],
                            q_a[:, 512:1024]), "sST", 16)
                    GP(_)

        def _(sync):
            sync.wait_ge(sems["sST"], cnt["sST"])
        SY(_)

    st.close()
    return nc


_NC_CACHE = None


def kernel(L0_attn_hidden_states, L0_attn_in_proj_w, L0_attn_dt_in_proj_w,
           L0_attn_dt_proj_w, L0_attn_dt_proj_b, L0_attn_conv_w, L0_attn_conv_b,
           L0_attn_A_log, L0_attn_D, L0_attn_out_proj_w, L0_ffn_weight,
           L0_ffn_bias, L0_ffn_x_tensor_scale):
    global _NC_CACHE
    hs = np.ascontiguousarray(np.asarray(L0_attn_hidden_states, np.float32))
    ipw = np.asarray(L0_attn_in_proj_w, np.float32)
    dtin_t = _tile_lhsT(np.asarray(L0_attn_dt_in_proj_w, np.float32))[0]
    dtpw = np.asarray(L0_attn_dt_proj_w, np.float32)
    dtbv = np.asarray(L0_attn_dt_proj_b, np.float32)
    convw = np.asarray(L0_attn_conv_w, np.float32)
    convbv = np.asarray(L0_attn_conv_b, np.float32)
    alog = np.asarray(L0_attn_A_log, np.float32)
    dvec = np.asarray(L0_attn_D, np.float32)
    wo = np.asarray(L0_attn_out_proj_w, np.float32)
    wffn = np.ascontiguousarray(
        np.asarray(L0_ffn_weight, np.float32).reshape(KT, 128, D_MODEL))
    ffnb = np.asarray(L0_ffn_bias, np.float32).reshape(1, D_MODEL)
    xts = np.asarray(L0_ffn_x_tensor_scale, np.float32).reshape(1, 1)

    base = dict(
        dtin_t=dtin_t, ffnb=ffnb, xts=xts,
        bsel=_b_sel(), csel=_c_sel(),
        ones_row=np.ones((128, T_C), ml_dtypes.bfloat16),
        identb=np.eye(128, dtype=ml_dtypes.bfloat16),
        identf=np.eye(128, dtype=np.float32),
    )
    in_maps = []
    for c in range(8):
        b, h = c // 2, c % 2
        m_wffn = wffn[2 * c:2 * c + 2]
        ch = slice(D_LOC * h, D_LOC * (h + 1))
        xb = slice(512 * h, 512 * (h + 1))
        m = dict(base)
        wt_loc = np.concatenate([
            ipw[0:D_INNER][ch],
            ipw[D_INNER:D_INNER + 1024][xb],
            ipw[D_INNER + 1024:D_INNER + 2048][xb],
            ipw[D_INNER + 2048:][ch],
        ], axis=0)
        m["wt_all"] = _tile_lhsT(wt_loc)
        dtp = _tile_lhsT(dtpw[ch])
        m["dtp_t"] = np.ascontiguousarray(
            dtp.transpose(1, 0, 2).reshape(128, G * 128))
        m["dtb"] = dtbv[ch].reshape(1, D_LOC)
        m["convsel"] = _conv_sel(convw[ch]).reshape(G * 4, 128, 128)
        m["convb"] = convbv[ch].reshape(1, D_LOC)
        m["alog_t"] = np.ascontiguousarray(
            alog[ch].reshape(G, 128, D_STATE).transpose(1, 0, 2)
            .reshape(128, G * D_STATE))
        m["dvec_t"] = np.ascontiguousarray(dvec[ch].reshape(G, 128).T)
        m["wo_t"] = _tile_lhsT(np.ascontiguousarray(wo[:, ch]))
        m["wffn"] = np.ascontiguousarray(m_wffn)
        m["hs"] = hs[b]
        m["hs2"] = np.ascontiguousarray(
            hs[b].reshape(N_CHUNK, 2, T_M, D_MODEL)[:, h]
            .reshape(N_CHUNK * T_M, D_MODEL))
        in_maps.append(m)

    if _NC_CACHE is None:
        _NC_CACHE = build_program()
    res = run_bass_kernel_spmd(_NC_CACHE, in_maps, core_ids=list(range(8)))
    out = np.zeros((4, L, D_MODEL), np.float32)
    for c in range(8):
        b, h = c // 2, c % 2
        r = np.asarray(res.results[c]["out_tm"]).reshape(N_CHUNK, T_M, D_MODEL)
        out[b].reshape(N_CHUNK, 2, T_M, D_MODEL)[:, h] = r
    return out


# version 13 (final)
# speedup vs baseline: 1.0506x; 1.0506x over previous
import numpy as np
import ml_dtypes
from contextlib import ExitStack

import concourse.bass as bass
import concourse.mybir as mybir
from concourse.bass_utils import run_bass_kernel_spmd

F32 = mybir.dt.float32
BF16 = mybir.dt.bfloat16
AF = mybir.ActivationFunctionType
ALU = mybir.AluOpType
AX = mybir.AxisListType.X

D_MODEL = 2048
D_INNER = 4096
D_LOC = 2048
D_STATE = 16
L = 4096

T_C = 512
N_CHUNK = L // T_C
G = D_LOC // 128
KT = D_MODEL // 128
N_MI = 40
MI_Z, MI_X, MI_B, MI_C = 0, 16, 20, 24
NTT = T_C // 128
T_M = T_C // 2
NTT2 = T_M // 128
CC_GROUPS = [[0, 1], [2, 3], [4, 5], [6, 7]]
LN_EPS = 1e-5
R2688 = float(np.float32(1.0) / np.float32(2688.0))
NBATCH = 4
RACE_SAFE = True
SEG_DRAINS = True
CC_FAKE = False
DEBUG_CHUNKS = None


def _tile_lhsT(w):
    M, K = w.shape
    t = w.reshape(M // 128, 128, K // 128, 128).transpose(0, 3, 2, 1)
    return np.ascontiguousarray(t.reshape(M // 128, 128, K))


def _conv_sel(conv_w):
    out = np.zeros((G, 4, 128, 128), np.float32)
    ch = np.arange(D_LOC)
    h, j = ch // 16, ch % 16
    xch = (h // 4) * 16 + j
    for g in range(G):
        for m in range(128):
            c = 128 * g + m
            out[g, :, xch[c] - 128 * (g // 4), m] = conv_w[c, 0, :]
    return out


def _b_sel():
    out = np.zeros((4, D_STATE, 128, 128), np.float32)
    for q in range(4):
        for n in range(D_STATE):
            out[q, n, 32 * q + n, 0:64] = -1.0
            out[q, n, 32 * q + 16 + n, 64:128] = -1.0
    return out.astype(ml_dtypes.bfloat16).reshape(4 * D_STATE, 128, 128)


def _c_sel():
    out = np.zeros((D_STATE, 128, 128), np.float32)
    for n in range(D_STATE):
        for h in range(8):
            out[n, 16 * h + n, 16 * h:16 * (h + 1)] = 1.0
    return out.astype(ml_dtypes.bfloat16)


def build_program():
    nc = bass.Bass("TRN2", target_bir_lowering=False,
                  detect_race_conditions=RACE_SAFE)
    st = ExitStack()
    e = st.enter_context

    hs_d = nc.dram_tensor("hs", [L, D_MODEL], F32, kind="ExternalInput").ap()
    hs2_d = nc.dram_tensor("hs2", [N_CHUNK * T_M, D_MODEL], F32, kind="ExternalInput").ap()
    wt_all = nc.dram_tensor("wt_all", [N_MI, 128, D_MODEL], F32, kind="ExternalInput").ap()
    dtin_d = nc.dram_tensor("dtin_t", [128, D_MODEL], F32, kind="ExternalInput").ap()
    dtp_d = nc.dram_tensor("dtp_t", [128, G * 128], F32, kind="ExternalInput").ap()
    dtb_d = nc.dram_tensor("dtb", [1, D_LOC], F32, kind="ExternalInput").ap()
    convsel_d = nc.dram_tensor("convsel", [G * 4, 128, 128], F32, kind="ExternalInput").ap()
    convb_d = nc.dram_tensor("convb", [1, D_LOC], F32, kind="ExternalInput").ap()
    alog_d = nc.dram_tensor("alog_t", [128, G * D_STATE], F32, kind="ExternalInput").ap()
    dvec_d = nc.dram_tensor("dvec_t", [128, G], F32, kind="ExternalInput").ap()
    wo_d = nc.dram_tensor("wo_t", [KT, 128, D_LOC], F32, kind="ExternalInput").ap()
    wffn_d = nc.dram_tensor("wffn", [2, 128, D_MODEL], F32, kind="ExternalInput").ap()
    ffnb_d = nc.dram_tensor("ffnb", [1, D_MODEL], F32, kind="ExternalInput").ap()
    xts_d = nc.dram_tensor("xts", [1, 1], F32, kind="ExternalInput").ap()
    bsel_d = nc.dram_tensor("bsel", [4 * D_STATE, 128, 128], BF16, kind="ExternalInput").ap()
    csel_d = nc.dram_tensor("csel", [D_STATE, 128, 128], BF16, kind="ExternalInput").ap()
    ones_d = nc.dram_tensor("ones_row", [128, T_C], BF16, kind="ExternalInput").ap()
    identb_d = nc.dram_tensor("identb", [128, 128], BF16, kind="ExternalInput").ap()
    identf_d = nc.dram_tensor("identf", [128, 128], F32, kind="ExternalInput").ap()

    out_d = nc.dram_tensor("out_tm", [N_CHUNK * T_M, D_MODEL], F32, kind="ExternalOutput").ap()

    wt_bf = nc.dram_tensor("wt_bf", [N_MI + 1, 128, D_MODEL], BF16).ap()
    wo_bf = nc.dram_tensor("wo_bf", [KT, 128, D_LOC], BF16).ap()
    agin_d = nc.dram_tensor("agin", [2 * KT * 128, 128], BF16).ap()
    agout_d = nc.dram_tensor("agout", [8 * 2 * KT * 128, 128], BF16).ap()
    agin4 = agin_d.rearrange("(a b p) j -> a b p j", b=KT, p=128)
    agout5 = agout_d.rearrange("(c a b p) j -> c a b p j", c=8, b=KT, p=128)
    amax_d = nc.dram_tensor("amax_d", [1, 16], F32).ap()
    amax2_d = nc.dram_tensor("amax2_d", [1, 16], F32).ap()
    cvs_bf = nc.dram_tensor("cvs_bf", [G, 128, 4 * 128], BF16).ap()
    gath_d = nc.dram_tensor("gath_d", [128, 1], F32).ap()
    sca_d = nc.dram_tensor("sca_d", [1, 8], F32).ap()
    bhi_d = nc.dram_tensor("bhi_d", [2, D_LOC], BF16).ap()
    cc_in = nc.dram_tensor("cc_in", [2, T_C, D_MODEL], F32).ap()
    cc_out = nc.dram_tensor("cc_out", [2, T_M, D_MODEL], F32).ap()

    bsel_sb = e(nc.sbuf_tensor("bsel_sb", [128, 4 * D_STATE, 128], BF16))
    csel_sb = e(nc.sbuf_tensor("csel_sb", [128, D_STATE, 128], BF16))
    dtp_sb = e(nc.sbuf_tensor("dtp_sb", [128, G, 128], BF16))
    a_scan = e(nc.sbuf_tensor("a_scan", [128, G * D_STATE], F32))
    d_sb = e(nc.sbuf_tensor("d_sb", [128, G], F32))
    dtbK = e(nc.sbuf_tensor("dtbK", [128, G, 128], BF16))
    cvbK = e(nc.sbuf_tensor("cvbK", [128, G, 128], BF16))
    ones_sb = e(nc.sbuf_tensor("ones_sb", [128, T_C], BF16))
    identb_sb = e(nc.sbuf_tensor("identb_sb", [128, 128], BF16))
    identf_sb = e(nc.sbuf_tensor("identf_sb", [128, 128], F32))
    ffnb_bc = e(nc.sbuf_tensor("ffnb_bc", [128, D_MODEL], F32))
    colbuf = e(nc.sbuf_tensor("colbuf", [128, 8], F32))
    smalls = e(nc.sbuf_tensor("smalls", [1, 16], F32))
    gath = e(nc.sbuf_tensor("gath", [1, 128], F32))
    carries = e(nc.sbuf_tensor("carries", [128, G * D_STATE], F32))
    lncols = e(nc.sbuf_tensor("lncols", [128, 12], F32))

    stagebf = e(nc.sbuf_tensor("stagebf", [128, 2048], BF16))
    cvslab = e(nc.sbuf_tensor("cvslab", [128, 1, 4 * 128], BF16))

    arena1 = e(nc.sbuf_tensor("arena1", [128, 2048], F32))
    hsn_t = e(nc.sbuf_tensor("hsn_t", [128, KT, T_C], BF16))
    x_bf = e(nc.sbuf_tensor("x_bf", [128, 4, 3 + T_C], BF16))
    b_bf = e(nc.sbuf_tensor("b_bf", [128, 8, T_C], BF16))
    dtlow_bf = e(nc.sbuf_tensor("dtlow_bf", [128, T_C], BF16))
    t2_g = e(nc.sbuf_tensor("t2_g", [128, T_C], F32))
    u_g = e(nc.sbuf_tensor("u_g", [128, T_C], F32))
    pi_g = e(nc.sbuf_tensor("pi_g", [128, T_C], F32))
    zsil_g = e(nc.sbuf_tensor("zsil_g", [128, T_C], F32))
    cbf_g = e(nc.sbuf_tensor("cbf_g", [128, T_C], BF16))
    da_b = e(nc.sbuf_tensor("da_b", [128, NBATCH, T_C], F32))
    db_b = e(nc.sbuf_tensor("db_b", [128, NBATCH, T_C], F32))
    s_b = e(nc.sbuf_tensor("s_b", [128, NBATCH, T_C], F32))
    sig_g = da_b[:, 0, :]

    y_g = e(nc.sbuf_tensor("y_g", [128, T_C], F32))
    y2s = e(nc.sbuf_tensor("y2s", [128, T_C], F32))
    yt_s = e(nc.sbuf_tensor("yt_s", [128, T_C], F32))
    ygate = e(nc.sbuf_tensor("ygate", [128, G, T_C], BF16))
    x2_tm = e(nc.sbuf_tensor("x2_tm", [128, NTT2, D_MODEL], F32))
    ao_sb = e(nc.sbuf_tensor("ao_sb", [128, 2048], F32))
    xdq_t = e(nc.sbuf_tensor("xdq_t", [128, KT, 128 * NTT2], BF16))
    q_a = e(nc.sbuf_tensor("q_a", [128, D_MODEL], F32))
    q_b = e(nc.sbuf_tensor("q_b", [128, D_MODEL], F32))
    q_sg = arena1
    q_st = None
    qs = e(nc.sbuf_tensor("qs", [128, 768], F32))
    bsq = e(nc.sbuf_tensor("bsq", [128, 128], F32))
    rden = e(nc.sbuf_tensor("rden", [128, 128], F32))

    wslab = e(nc.sbuf_tensor("wslab", [128, 2, D_MODEL], BF16))
    woslab = x2_tm[:].bitcast(BF16)[:, :, 0:D_LOC]
    wqslab = None

    stage32 = x2_tm[:].rearrange("p a b -> p (a b)")[:, 0:2048]
    x2n = ao_sb


    wqslab = wslab
    ps_a = e(nc.psum_tensor("ps_a", [128, 512], F32))
    ps_b = e(nc.psum_tensor("ps_b", [128, 512], F32))
    ps_c = e(nc.psum_tensor("ps_c", [128, 512], F32))
    ps_d = e(nc.psum_tensor("ps_d", [128, 512], F32))
    ps_big = e(nc.psum_tensor("ps_big", [128, 2048], F32))
    psb16 = ps_big[:].bitcast(BF16)

    sems = {}
    cnt = {}

    with nc.Block() as blk:
        for s in ["sLD", "sST", "sPE", "sACT", "sDVE", "sGP", "sC", "sWL", "sHS", "sCV", "sWS0", "sWS1", "sWO0", "sWO1", "sCI", "sCC"]:
            sems[s] = e(nc.semaphore(s))
            cnt[s] = 0

        def inc(bi, name, by=1):
            cnt[name] += by
            return bi.then_inc(sems[name], by)

        SY, PE, SC, VE, GP = blk.sync, blk.tensor, blk.scalar, blk.vector, blk.gpsimd

        def pslot(i):
            return [ps_a, ps_b, ps_c, ps_d][i][:, 0:T_C]

        def join():
            snap = dict(cnt)
            for sec, skip in [(PE, "sPE"), (SC, "sACT"), (VE, "sDVE"),
                              (GP, "sGP"), (SY, None)]:
                def _(t, skip=skip, snap=snap):
                    for s in ["sPE", "sACT", "sDVE", "sGP", "sLD", "sST", "sC",
                              "sWL", "sHS", "sCV", "sWS0", "sWS1", "sWO0",
                              "sWO1", "sCI", "sCC"]:
                        if s != skip and snap[s]:
                            t.wait_ge(sems[s], snap[s])
                sec(_)

        def _(sync):
            for dst, src in [
                (bsel_sb[:], bsel_d.rearrange("a p b -> p a b")),
                (csel_sb[:], csel_d.rearrange("a p b -> p a b")),
                (a_scan[:], alog_d), (d_sb[:], dvec_d),
                (ones_sb[:], ones_d), (identb_sb[:], identb_d), (identf_sb[:], identf_d),
                (ffnb_bc[:], ffnb_d[0:1, :].broadcast_to([128, D_MODEL])),
                (smalls[0:1, 0:1], xts_d),
                (q_a[0:1, 0:D_LOC], dtb_d),
                (q_b[0:1, 0:D_LOC], convb_d),
            ]:
                inc(sync.dma_start(dst, src), "sC", 16)
        SY(_)

        def _(vector):
            vector.wait_ge(sems["sC"], cnt["sC"])
            vector.memset(carries[:], 0.0)
            if RACE_SAFE:
                vector.drain()
            vector.memset(smalls[0:1, 1:16], 0.0)
            if RACE_SAFE:
                vector.drain()
            vector.memset(x_bf[:].rearrange("p a b -> p (a b)"), 0.0)
            if RACE_SAFE:
                vector.drain()
            vector.memset(lncols[:, 9:10], LN_EPS)
            if RACE_SAFE:
                vector.drain()
            vector.memset(dtbK[:].rearrange("p a b -> p (a b)"), 0.0)
            if RACE_SAFE:
                vector.drain()
            vector.memset(cvbK[:].rearrange("p a b -> p (a b)"), 0.0)
            if RACE_SAFE:
                vector.drain()
            for i, src in enumerate([q_a, q_b]):
                vector.tensor_copy(stagebf[0:1, 0:D_LOC], src[0:1, 0:D_LOC])
                if RACE_SAFE:
                    vector.drain()
                vector.tensor_copy(q_sg[0:1, 0:D_LOC], stagebf[0:1, 0:D_LOC])
                if RACE_SAFE:
                    vector.drain()
                vector.tensor_sub(q_sg[32:33, 0:D_LOC], src[0:1, 0:D_LOC],
                                  q_sg[0:1, 0:D_LOC])
                if RACE_SAFE:
                    vector.drain()
                vector.tensor_copy(stagebf[32 + 32 * i:33 + 32 * i, 0:D_LOC],
                                   q_sg[32:33, 0:D_LOC])
                if RACE_SAFE:
                    vector.drain()
            inc(vector.drain(), "sDVE")
        VE(_)

        def _(gpsimd):
            gpsimd.wait_ge(sems["sDVE"], cnt["sDVE"])
            inc(gpsimd.dma_start(bhi_d[0:1, :], dtb_d), "sST", 16)
            inc(gpsimd.dma_start(bhi_d[1:2, :], convb_d), "sST", 16)
            gpsimd.wait_ge(sems["sST"], cnt["sST"])
            for i, dst in enumerate([dtbK, cvbK]):
                inc(gpsimd.dma_start(
                    dst[0:1, :, :].rearrange("p a b -> p (a b)"), bhi_d[i:i + 1, :]),
                    "sST", 16)
                inc(gpsimd.dma_start(
                    dst[1:2, :, :].rearrange("p a b -> p (a b)"),
                    stagebf[32 + 32 * i:33 + 32 * i, 0:D_LOC]),
                    "sST", 16)
        GP(_)

        def _(scalar):
            scalar.wait_ge(sems["sC"], cnt["sC"])
            scalar.activation(a_scan[:], a_scan[:], AF.Exp)
            inc(scalar.drain(), "sACT")
        SC(_)

        conv_items = [(wt_all[mi], wt_bf[mi]) for mi in range(N_MI)]
        conv_items.append((dtin_d, wt_bf[N_MI]))
        conv_items.append((wo_d.rearrange("k p b -> p k b"),
                           wo_bf.rearrange("k p b -> p k b")))
        for gi in range(G):
            conv_items.append((
                convsel_d[4 * gi:4 * (gi + 1)].rearrange("k p b -> p k b"),
                cvs_bf[gi].rearrange("p (k b) -> p k b", b=128)))
        conv_items.append((dtp_d, dtp_sb[:].rearrange("p a b -> p (a b)")))
        def _(gpsimd):
            for src, dst in conv_items:
                inc(gpsimd.dma_start(dst, src), "sST", 16)
        GP(_)

        def _(vector):
            vector.wait_ge(sems["sST"], cnt["sST"])
            vector.memset(lncols[:, 0:1], 0.0)
            if RACE_SAFE:
                vector.drain()
            inc(vector.drain(), "sDVE")
        VE(_)

        for ti in range(2):
            def _(sync, ti=ti):
                sync.wait_ge(sems["sDVE"], cnt["sDVE"])
                inc(sync.dma_start(stage32[:], wffn_d[ti]), "sWL", 16)
            SY(_)

            def _(vector):
                vector.wait_ge(sems["sWL"], cnt["sWL"])
                vector.tensor_reduce(lncols[:, 1:2], stage32[:], AX, ALU.max,
                                     apply_absolute_value=True)
                if RACE_SAFE:
                    vector.drain()
                vector.tensor_max(lncols[:, 0:1], lncols[:, 0:1], lncols[:, 1:2])
                if RACE_SAFE:
                    vector.drain()
                inc(vector.drain(), "sDVE")
            VE(_)

        def _(gpsimd):
            gpsimd.wait_ge(sems["sDVE"], cnt["sDVE"])
            inc(gpsimd.dma_start(gath_d, lncols[:, 0:1]), "sST", 16)
        GP(_)

        def _(sync):
            sync.wait_ge(sems["sST"], cnt["sST"])
            inc(sync.dma_start(gath[:], gath_d.rearrange("a b -> b a")), "sWL", 16)
        SY(_)

        def _(vector):
            vector.wait_ge(sems["sWL"], cnt["sWL"])
            inc(vector.tensor_reduce(smalls[0:1, 1:2], gath[:], AX, ALU.max),
                "sDVE")
        VE(_)

        def _(gpsimd):
            gpsimd.wait_ge(sems["sDVE"], cnt["sDVE"])
            inc(gpsimd.dma_start(amax_d, smalls[0:1, 0:16]), "sST", 16)
            gpsimd.wait_ge(sems["sST"], cnt["sST"])
            inc(gpsimd.collective_compute(
                "AllReduce", ALU.max, replica_groups=[list(range(8))],
                ins=[amax_d.opt()], outs=[amax2_d.opt()]), "sCC")
        GP(_)

        def _(sync):
            sync.wait_ge(sems["sCC"], cnt["sCC"])
            inc(sync.dma_start(smalls[0:1, 1:2], amax2_d[0:1, 1:2]), "sWL", 16)
        SY(_)

        def _(vector):
            vector.wait_ge(sems["sWL"], cnt["sWL"])
            vector.tensor_scalar_mul(smalls[0:1, 2:3], smalls[0:1, 1:2], R2688)
            if RACE_SAFE:
                vector.drain()
            vector.tensor_scalar_mul(smalls[0:1, 3:4], smalls[0:1, 2:3], 6.0)
            if RACE_SAFE:
                vector.drain()
            vector.reciprocal(smalls[0:1, 4:5], smalls[0:1, 3:4])
            if RACE_SAFE:
                vector.drain()
            vector.tensor_mul(smalls[0:1, 5:6], smalls[0:1, 2:3], smalls[0:1, 0:1])
            if RACE_SAFE:
                vector.drain()
            vector.tensor_scalar_mul(smalls[0:1, 6:7], smalls[0:1, 0:1], 6.0)
            if RACE_SAFE:
                vector.drain()
            vector.reciprocal(smalls[0:1, 7:8], smalls[0:1, 6:7])
            if RACE_SAFE:
                vector.drain()
            inc(vector.drain(), "sDVE")
        VE(_)

        def _(gpsimd):
            gpsimd.wait_ge(sems["sDVE"], cnt["sDVE"])
            inc(gpsimd.dma_start(sca_d, smalls[0:1, 0:8]), "sST", 16)
        GP(_)

        def _(sync):
            sync.wait_ge(sems["sST"], cnt["sST"])
            for col, idx in [(0, 5), (1, 7), (2, 4), (3, 2), (4, 0)]:
                inc(sync.dma_start(colbuf[:, col:col + 1],
                                   sca_d[0:1, idx:idx + 1].broadcast_to([128, 1])),
                    "sWL", 16)
        SY(_)

        I32 = mybir.dt.int32

        def emit_bs_quant(vector, v3, W, r6col, tscol):
            NB = W // 16
            c0, c1, c2 = (qs[:, i * NB:(i + 1) * NB] for i in range(3))
            iv1 = qs[:, 3 * NB:4 * NB].bitcast(I32)
            iv2 = qs[:, 4 * NB:5 * NB].bitcast(I32)
            vector.tensor_reduce(c0, v3, AX, ALU.max, apply_absolute_value=True)
            if RACE_SAFE:
                vector.drain()
            vector.scalar_tensor_tensor(c0, c0, 1.0, r6col.broadcast_to([128, NB]),
                                        ALU.mult, ALU.mult)
            if RACE_SAFE:
                vector.drain()
            vector.tensor_scalar_min(c0, c0, 448.0)
            if RACE_SAFE:
                vector.drain()
            vector.tensor_scalar(iv1, c0.bitcast(I32), 23, None,
                                 ALU.logical_shift_right)
            if RACE_SAFE:
                vector.drain()
            vector.tensor_scalar(iv1, iv1, 23, None, ALU.logical_shift_left)
            if RACE_SAFE:
                vector.drain()
            vector.tensor_scalar_max(c1, iv1.bitcast(F32), float(2.0 ** -6))
            if RACE_SAFE:
                vector.drain()
            vector.reciprocal(c2, c1)
            if RACE_SAFE:
                vector.drain()
            vector.scalar_tensor_tensor(c2, c0, 8.0, c2, ALU.mult, ALU.mult)
            if RACE_SAFE:
                vector.drain()
            vector.tensor_scalar_add(c2, c2, float(2.0 ** 23))
            if RACE_SAFE:
                vector.drain()
            vector.tensor_scalar_add(c2, c2, -float(2.0 ** 23))
            if RACE_SAFE:
                vector.drain()
            vector.scalar_tensor_tensor(bsq[:, 0:NB], c2, 0.125, c1, ALU.mult, ALU.mult)
            if RACE_SAFE:
                vector.drain()
            vector.tensor_scalar(c0, bsq[:, 0:NB], 0.0, None, ALU.is_equal)
            if RACE_SAFE:
                vector.drain()
            vector.tensor_add(c0, c0, bsq[:, 0:NB])
            if RACE_SAFE:
                vector.drain()
            vector.scalar_tensor_tensor(c0, c0, 1.0, tscol.broadcast_to([128, NB]),
                                        ALU.mult, ALU.mult)
            if RACE_SAFE:
                vector.drain()
            vector.reciprocal(rden[:, 0:NB], c0)
            if RACE_SAFE:
                vector.drain()

        def emit_xq(vector, v_ap, W, out_bf):
            NB = W // 16
            rb = rden[:, 0:NB].unsqueeze(2).to_broadcast([128, NB, 16])
            vector.tensor_scalar_mul(q_sg[:, 0:W], v_ap, -1.0)
            if RACE_SAFE:
                vector.drain()
            vector.tensor_max(q_sg[:, 0:W], q_sg[:, 0:W], v_ap)
            if RACE_SAFE:
                vector.drain()
            vector.tensor_tensor(q_a[:, 0:W].rearrange("p (a b) -> p a b", b=16),
                                 q_sg[:, 0:W].rearrange("p (a b) -> p a b", b=16),
                                 rb, ALU.mult)
            if RACE_SAFE:
                vector.drain()
            a = q_a[:, 0:W]
            acc = q_b[:, 0:W]
            vector.tensor_scalar(acc, a, 0.25, 1.0, ALU.is_gt, ALU.mult)
            if RACE_SAFE:
                vector.drain()
            for thr, op, rep in [(0.75, ALU.is_ge, 1), (1.25, ALU.is_gt, 1),
                                 (1.75, ALU.is_ge, 1), (2.5, ALU.is_gt, 2),
                                 (3.5, ALU.is_ge, 2), (5.0, ALU.is_gt, 4)]:
                for _r in range(rep):
                    vector.scalar_tensor_tensor(acc, a, thr, acc, op, ALU.add)
                    if RACE_SAFE:
                        vector.drain()
            vector.tensor_scalar(q_sg[:, 0:W], v_ap, 0.0, -2.0, ALU.is_lt, ALU.mult)
            if RACE_SAFE:
                vector.drain()
            vector.tensor_scalar_add(q_sg[:, 0:W], q_sg[:, 0:W], 1.0)
            if RACE_SAFE:
                vector.drain()
            bb = bsq[:, 0:NB].unsqueeze(2).to_broadcast([128, NB, 16])
            vector.scalar_tensor_tensor(q_sg[:, 0:W].rearrange("p (a b) -> p a b", b=16),
                                        q_sg[:, 0:W].rearrange("p (a b) -> p a b", b=16),
                                        0.5, bb, ALU.mult, ALU.mult)
            if RACE_SAFE:
                vector.drain()
            vector.tensor_mul(out_bf, acc, q_sg[:, 0:W])
            if RACE_SAFE:
                vector.drain()

        for ti in range(2):
            def _(sync, ti=ti):
                sync.wait_ge(sems["sST"], cnt["sST"])
                sync.wait_ge(sems["sDVE"], cnt["sDVE"])
                inc(sync.dma_start(stage32[:], wffn_d[ti]), "sWL", 16)
            SY(_)

            def _(vector, ti=ti):
                vector.wait_ge(sems["sWL"], cnt["sWL"])
                vector.wait_ge(sems["sACT"], cnt["sACT"])
                emit_bs_quant(vector, stage32.rearrange("p (a b) -> p a b", b=16),
                              2048, colbuf[:, 2:3], colbuf[:, 3:4])
                emit_xq(vector, stage32, 2048, stagebf[:])
                inc(vector.drain(), "sDVE")
            VE(_)

            def _(tensor, ti=ti):
                tensor.wait_ge(sems["sDVE"], cnt["sDVE"])
                tensor.wait_ge(sems["sACT"], cnt["sACT"])
                bi = None
                for kd in range(KT):
                    bi = tensor.transpose(psb16[:, 128 * kd:128 * (kd + 1)],
                                          stagebf[:, 128 * kd:128 * (kd + 1)],
                                          identb_sb[:])
                inc(bi, "sPE")
            PE(_)

            def _(scalar, ti=ti):
                scalar.wait_ge(sems["sPE"], cnt["sPE"])
                scalar.wait_ge(sems["sDVE"], cnt["sDVE"])
                inc(scalar.activation(stagebf[:], psb16[:, 0:2048], AF.Copy), "sACT")
            SC(_)

            def _(gpsimd, ti=ti):
                gpsimd.wait_ge(sems["sACT"], cnt["sACT"])
                for kd in range(KT):
                    inc(gpsimd.dma_start(
                        agin4[ti, kd],
                        stagebf[:, 128 * kd:128 * (kd + 1)]), "sST", 16)
            GP(_)

        def _(gpsimd):
            gpsimd.wait_ge(sems["sST"], cnt["sST"])
            inc(gpsimd.collective_compute(
                "AllGather", ALU.bypass, replica_groups=[list(range(8))],
                ins=[agin_d.opt()], outs=[agout_d.opt()]), "sCC")
        GP(_)

        wpar = [0]
        wfree = {0: 0, 1: 0}
        wopar = [0]
        wofree = {0: 0, 1: 0}
        wqpar = [0]
        wqfree = {0: 0, 1: 0}
        cvpar = [0]
        cvfree = {0: 0, 1: 0}

        def load_wslab(mi):
            p = wpar[0]
            wpar[0] ^= 1
            sname = "sWS%d" % p

            def _(sync, mi=mi, p=p, need=wfree[p], sname=sname):
                if need:
                    sync.wait_ge(sems["sPE"], need)
                inc(sync.dma_start(wslab[:, p], wt_bf[mi]), sname, 16)
            SY(_)
            return p, (sname, cnt[sname])

        def mm_group(psum_ap, lhsT_list, rhs_list, extra_waits=()):
            def _(tensor, psum_ap=psum_ap, lhsT_list=list(lhsT_list),
                  rhs_list=list(rhs_list), extra=list(extra_waits)):
                for s, v in extra:
                    if v:
                        tensor.wait_ge(sems[s], v)
                n = len(lhsT_list)
                bi = None
                for i in range(n):
                    bi = tensor.matmul(psum_ap, lhsT_list[i], rhs_list[i],
                                       start=(i == 0), stop=(i == n - 1))
                inc(bi, "sPE")
            PE(_)
            return cnt["sPE"]

        for ci in range(DEBUG_CHUNKS or N_CHUNK):
            t0 = ci * T_C
            join()

            for ti in range(NTT):
                def _(sync, ti=ti):
                    if ti == 0:
                        pass
                    else:
                        sync.wait_ge(sems["sACT"], cnt["sACT"])
                    inc(sync.dma_start(
                        arena1[:], hs_d[t0 + 128 * ti:t0 + 128 * (ti + 1), :]),
                        "sHS", 16)
                SY(_)

                def _(scalar, ti=ti):
                    scalar.wait_ge(sems["sHS"], cnt["sHS"])
                    scalar.wait_ge(sems["sDVE"], cnt["sDVE"])
                    bi = scalar.activation(ao_sb[:, 0:2048], arena1[:], AF.Identity,
                                           accum_out=lncols[:, 0:1])
                    bi = scalar.activation(q_b[:, 0:2048], arena1[:], AF.Square,
                                           accum_out=lncols[:, 1:2])
                    inc(bi, "sACT")
                SC(_)

                def _(vector, ti=ti):
                    vector.wait_ge(sems["sACT"], cnt["sACT"])
                    vector.tensor_scalar_mul(lncols[:, 2:3], lncols[:, 0:1],
                                             1.0 / D_MODEL)
                    vector.tensor_scalar_mul(lncols[:, 3:4], lncols[:, 1:2],
                                             1.0 / D_MODEL)
                    if RACE_SAFE:
                        vector.drain()
                    vector.tensor_mul(lncols[:, 4:5], lncols[:, 2:3], lncols[:, 2:3])
                    if RACE_SAFE:
                        vector.drain()
                    inc(vector.tensor_sub(lncols[:, 5:6], lncols[:, 3:4],
                                          lncols[:, 4:5]), "sDVE")
                VE(_)

                def _(scalar, ti=ti):
                    scalar.wait_ge(sems["sDVE"], cnt["sDVE"])
                    scalar.activation(lncols[:, 8:9], lncols[:, 5:6], AF.Ln,
                                      bias=lncols[:, 9:10])
                    scalar.drain()
                    inc(scalar.activation(lncols[:, 8:9], lncols[:, 8:9], AF.Exp,
                                          scale=0.5), "sACT")
                SC(_)

                def _(vector, ti=ti):
                    vector.wait_ge(sems["sACT"], cnt["sACT"])
                    vector.reciprocal(lncols[:, 6:7], lncols[:, 8:9])
                    if RACE_SAFE:
                        vector.drain()
                    inc(vector.scalar_tensor_tensor(lncols[:, 7:8], lncols[:, 2:3],
                                                    -1.0, lncols[:, 6:7], ALU.mult,
                                                    ALU.mult), "sDVE")
                VE(_)

                def _(scalar, ti=ti):
                    scalar.wait_ge(sems["sDVE"], cnt["sDVE"])
                    inc(scalar.activation(stagebf[:], arena1[:], AF.Identity,
                                          scale=lncols[:, 6:7], bias=lncols[:, 7:8]),
                        "sACT")
                SC(_)

                def _(tensor, ti=ti):
                    tensor.wait_ge(sems["sACT"], cnt["sACT"])
                    bi = None
                    for kt in range(KT):
                        bi = tensor.transpose(psb16[:, 128 * kt:128 * (kt + 1)],
                                              stagebf[:, 128 * kt:128 * (kt + 1)],
                                              identb_sb[:])
                    inc(bi, "sPE")
                PE(_)

                def _(scalar, ti=ti):
                    scalar.wait_ge(sems["sPE"], cnt["sPE"])
                    inc(scalar.activation(
                        hsn_t[:, :, 128 * ti:128 * (ti + 1)],
                        psb16[:, 0:2048].rearrange("p (a b) -> p a b", b=128),
                        AF.Copy), "sACT")
                SC(_)

            if ci > 0:
                def _(vector):
                    for j in range(4):
                        vector.tensor_copy(x_bf[:, j, 0:3], x_bf[:, j, T_C:T_C + 3])
                        if RACE_SAFE:
                            vector.drain()
                    inc(vector.drain(), "sDVE")
                VE(_)
            xtail_dve = cnt["sDVE"]

            psx = [ps_a, ps_b, ps_c, ps_d]
            for j, mi in enumerate(list(range(MI_X, MI_C)) + [-1]):
                bank = psx[j % 4][:, 0:T_C]
                p, ldv = load_wslab(mi if mi >= 0 else N_MI)
                mm_group(bank,
                         [wslab[:, p, 128 * kt:128 * (kt + 1)] for kt in range(KT)],
                         [hsn_t[:, kt, :] for kt in range(KT)],
                         [ldv, ("sACT", cnt["sACT"]), ("sDVE", cnt["sDVE"])])
                wfree[p] = cnt["sPE"]

                def _(scalar, mi=mi, bank=bank, xtail_dve=xtail_dve):
                    scalar.wait_ge(sems["sPE"], cnt["sPE"])
                    if 0 <= mi < MI_B:
                        scalar.wait_ge(sems["sDVE"], xtail_dve)
                        dst = x_bf[:, mi - MI_X, 3:3 + T_C]
                    elif mi >= 0:
                        dst = b_bf[:, mi - MI_B, :]
                    else:
                        dst = dtlow_bf[:]
                    inc(scalar.activation(dst, bank, AF.Copy), "sACT")
                SC(_)

            slotfree = [0]
            for g in range(G):
                q4 = g % 4
                xt_j = g // 4
                cp = 0

                def _(sync, g=g, cp=cp, need=cvfree[cp]):
                    if need:
                        sync.wait_ge(sems["sPE"], need)
                    inc(sync.dma_start(cvslab[:, cp], cvs_bf[g]), "sCV", 16)
                SY(_)
                cv_ld = cnt["sCV"]

                mm_group(ps_a[:, 0:T_C],
                         [dtp_sb[:, g, :], dtbK[:, g, :]],
                         [dtlow_bf[:], ones_sb[:]],
                         [("sACT", cnt["sACT"]), ("sDVE", cnt["sDVE"])])

                def _(scalar, g=g, dvew=cnt["sDVE"]):
                    scalar.wait_ge(sems["sPE"], cnt["sPE"])
                    scalar.wait_ge(sems["sDVE"], dvew)
                    inc(scalar.activation(sig_g, ps_a[:, 0:T_C], AF.Sigmoid,
                                          scale=-1.0), "sACT")
                SC(_)

                mm_group(ps_b[:, 0:T_C],
                         [cvslab[:, cp, 128 * k:128 * (k + 1)] for k in range(4)] +
                         [cvbK[:, g, :]],
                         [x_bf[:, xt_j, k:k + T_C] for k in range(4)] + [ones_sb[:]],
                         [("sCV", cv_ld), ("sACT", cnt["sACT"]),
                          ("sDVE", cnt["sDVE"])])
                cvfree[cp] = cnt["sPE"]
                pe_conv = cnt["sPE"]

                p, ldv = load_wslab(MI_Z + g)
                mm_group(ps_c[:, 0:T_C],
                         [wslab[:, p, 128 * kt:128 * (kt + 1)] for kt in range(KT)],
                         [hsn_t[:, kt, :] for kt in range(KT)],
                         [ldv, ("sACT", cnt["sACT"]), ("sDVE", cnt["sDVE"])])
                wfree[p] = cnt["sPE"]
                pe_z = cnt["sPE"]

                p, ldv = load_wslab(MI_C + g)
                mm_group(ps_d[:, 0:T_C],
                         [wslab[:, p, 128 * kt:128 * (kt + 1)] for kt in range(KT)],
                         [hsn_t[:, kt, :] for kt in range(KT)],
                         [ldv, ("sACT", cnt["sACT"]), ("sDVE", cnt["sDVE"])])
                wfree[p] = cnt["sPE"]
                pe_c = cnt["sPE"]

                def _(scalar, g=g, pe_conv=pe_conv, pe_z=pe_z, pe_c=pe_c,
                      dvw=cnt["sDVE"]):
                    scalar.drain()
                    scalar.activation(t2_g[:], sig_g, AF.Ln)
                    scalar.drain()
                    scalar.wait_ge(sems["sPE"], pe_z)
                    if dvw:
                        scalar.wait_ge(sems["sDVE"], dvw)
                    scalar.activation(u_g[:], ps_b[:, 0:T_C], AF.Silu)
                    inc(scalar.activation(zsil_g[:], ps_c[:, 0:T_C], AF.Silu), "sACT")
                SC(_)
                act_uz = cnt["sACT"]

                def _(scalar, g=g, pe_c=pe_c):
                    scalar.wait_ge(sems["sPE"], pe_c)
                    scalar.activation(cbf_g[:], ps_d[:, 0:T_C], AF.Copy)
                    bi = None
                    for k in range(NBATCH):
                        bi = scalar.activation(
                            da_b[:, k, :], t2_g[:], AF.Exp,
                            scale=a_scan[:, 16 * g + 0 + k:16 * g + 0 + k + 1])
                    inc(bi, "sACT")
                SC(_)
                act_da = {0: cnt["sACT"]}

                def _(vector, g=g, act_uz=act_uz):
                    vector.wait_ge(sems["sACT"], act_uz)
                    inc(vector.tensor_mul(pi_g[:], t2_g[:], u_g[:]), "sDVE")
                VE(_)
                dve_pi = cnt["sDVE"]

                nbat = D_STATE // NBATCH
                seg_inc = {}
                for nb in range(nbat):
                    n0 = nb * NBATCH
                    sl0 = 0

                    def _(tensor, g=g, n0=n0, q4=q4, xt_j=xt_j, sl0=sl0,
                          free=slotfree[0], da=act_da[nb]):
                        if free:
                            tensor.wait_ge(sems["sDVE"], free)
                        tensor.wait_ge(sems["sACT"], da)
                        bi = None
                        for k in range(NBATCH):
                            bi = tensor.matmul(
                                ps_big[:, (sl0 + k) * T_C:(sl0 + k + 1) * T_C],
                                bsel_sb[:, 16 * q4 + n0 + k, :],
                                b_bf[:, xt_j, :], start=True, stop=True)
                        inc(bi, "sPE")
                    PE(_)
                    pe_bb = cnt["sPE"]

                    def _(vector, sl0=sl0, pe_bb=pe_bb, dvew=cnt["sDVE"]):
                        vector.wait_ge(sems["sPE"], pe_bb)
                        vector.wait_ge(sems["sDVE"], dvew)
                        bi = None
                        for k in range(NBATCH):
                            bi = vector.tensor_mul(
                                db_b[:, k, :], pi_g[:],
                                ps_big[:, (sl0 + k) * T_C:(sl0 + k + 1) * T_C])
                        inc(bi, "sDVE")
                    VE(_)
                    dve_db = cnt["sDVE"]

                    def _(tensor, n0=n0, sl0=sl0, dve_db=dve_db, pe_c=pe_c):
                        tensor.wait_ge(sems["sDVE"], dve_db)
                        bi = None
                        for k in range(NBATCH):
                            bi = tensor.matmul(
                                ps_big[:, (sl0 + k) * T_C:(sl0 + k + 1) * T_C],
                                csel_sb[:, n0 + k, :],
                                cbf_g[:], start=True, stop=True)
                        inc(bi, "sPE")
                    PE(_)
                    pe_cb = cnt["sPE"]

                    def _(vector, g=g, n0=n0):
                        if SEG_DRAINS:
                            vector.drain()
                        bi = None
                        for k in range(NBATCH):
                            n = n0 + k
                            bi = vector.tensor_tensor_scan(
                                s_b[:, k, :], da_b[:, k, :], db_b[:, k, :],
                                carries[:, 16 * g + n:16 * g + n + 1],
                                ALU.mult, ALU.add)
                        inc(bi, "sDVE")
                    VE(_)
                    dve_scan = cnt["sDVE"]

                    if nb + 1 < nbat:
                        def _(scalar, g=g, nb=nb, dve_scan=dve_scan):
                            scalar.wait_ge(sems["sDVE"], dve_scan)
                            bi = None
                            for k in range(NBATCH):
                                n = (nb + 1) * NBATCH + k
                                bi = scalar.activation(
                                    da_b[:, k, :], t2_g[:], AF.Exp,
                                    scale=a_scan[:, 16 * g + n:16 * g + n + 1])
                            inc(bi, "sACT")
                        SC(_)
                        act_da[nb + 1] = cnt["sACT"]

                    def _(vector, g=g, n0=n0, sl0=sl0, pe_cb=pe_cb):
                        if SEG_DRAINS:
                            vector.drain()
                        vector.tensor_copy(
                            carries[:, 16 * g + n0:16 * g + n0 + NBATCH],
                            s_b[:, :, T_C - 1:T_C].squeeze(2))
                        vector.wait_ge(sems["sPE"], pe_cb)
                        bi = None
                        for k in range(NBATCH):
                            bi = vector.tensor_mul(
                                db_b[:, k, :], s_b[:, k, :],
                                ps_big[:, (sl0 + k) * T_C:(sl0 + k + 1) * T_C])
                        inc(bi, "sDVE")
                    VE(_)
                    slotfree[0] = cnt["sDVE"]

                    def _(vector, nb=nb, gpw=cnt["sGP"]):
                        if SEG_DRAINS:
                            vector.drain()
                        if gpw:
                            vector.wait_ge(sems["sGP"], gpw)
                        vector.tensor_add(y2s[:], db_b[:, 0, :], db_b[:, 1, :])
                        vector.tensor_add(yt_s[:], db_b[:, 2, :], db_b[:, 3, :])
                        if RACE_SAFE:
                            vector.drain()
                        inc(vector.tensor_add(y2s[:], y2s[:], yt_s[:]), "sDVE")
                    VE(_)
                    seg_inc[nb] = cnt["sDVE"]

                    def _(gpsimd, nb=nb, v=cnt["sDVE"]):
                        gpsimd.wait_ge(sems["sDVE"], v)
                        gpsimd.drain()
                        if nb == 0:
                            bi = gpsimd.tensor_copy(y_g[:], y2s[:])
                        else:
                            bi = gpsimd.tensor_add(y_g[:], y_g[:], y2s[:])
                        inc(bi, "sGP")
                    GP(_)

                def _(vector, g=g, act_uz=act_uz, gpw=cnt["sGP"]):
                    vector.wait_ge(sems["sACT"], act_uz)
                    vector.wait_ge(sems["sGP"], gpw)
                    vector.scalar_tensor_tensor(y_g[:], u_g[:], d_sb[:, g:g + 1],
                                                y_g[:], ALU.mult, ALU.add)
                    if RACE_SAFE:
                        vector.drain()
                    inc(vector.tensor_mul(ygate[:, g, :], y_g[:], zsil_g[:]),
                        "sDVE")
                VE(_)
                gate_dve = cnt["sDVE"]

            join()
            par = ci % 2

            def wvbank(w, j):
                if w % 2 == 0:
                    return ps_big[:, 512 * j:512 * (j + 1)]
                return [ps_a, ps_b, ps_c, ps_d][j][:, 0:T_C]

            def wvtile(w, tl, j):
                if w % 2 == 0:
                    return ps_big[:, (tl * 4 + j) * 128:(tl * 4 + j + 1) * 128]
                return [ps_a, ps_b, ps_c, ps_d][tl][:, 128 * j:128 * (j + 1)]

            arena_evac = {}
            for w in range(4):
                for j in range(4):
                    m = 4 * w + j
                    pq = wopar[0]
                    wopar[0] ^= 1
                    wos = "sWO%d" % pq

                    def _(sync, m=m, pq=pq, need=wofree[pq], wos=wos):
                        if need:
                            sync.wait_ge(sems["sPE"], need)
                        inc(sync.dma_start(woslab[:, pq], wo_bf[m]), wos, 16)
                    SY(_)
                    wo_ld = cnt[wos]

                    def _(tensor, w=w, j=j, pq=pq, wo_ld=wo_ld, wos=wos,
                          psfree=arena_evac.get(w - 2, 0)):
                        tensor.wait_ge(sems[wos], wo_ld)
                        if psfree:
                            tensor.wait_ge(sems["sACT"], psfree)
                        bi = None
                        for kg in range(G):
                            bi = tensor.matmul(
                                wvbank(w, j), woslab[:, pq, 128 * kg:128 * (kg + 1)],
                                ygate[:, kg, :], start=(kg == 0), stop=(kg == G - 1))
                        inc(bi, "sPE")
                    PE(_)
                    wofree[pq] = cnt["sPE"]

                def _(scalar, w=w, pe=cnt["sPE"]):
                    scalar.wait_ge(sems["sPE"], pe)
                    for j in range(4):
                        scalar.activation(ao_sb[:, 512 * j:512 * (j + 1)],
                                          wvbank(w, j), AF.Copy)
                    inc(scalar.drain(), "sACT")
                SC(_)
                act_evac = cnt["sACT"]

                def _(tensor, w=w, act_evac=act_evac):
                    tensor.wait_ge(sems["sACT"], act_evac)
                    bi = None
                    for tl in range(NTT):
                        for j in range(4):
                            bi = tensor.transpose(
                                wvtile(w, tl, j),
                                ao_sb[:, 512 * j + 128 * tl:512 * j + 128 * (tl + 1)],
                                identf_sb[:])
                    inc(bi, "sPE")
                PE(_)
                pe_tr = cnt["sPE"]

                def _(scalar, w=w, pe_tr=pe_tr, cis=cnt["sCI"]):
                    scalar.wait_ge(sems["sPE"], pe_tr)
                    if cis:
                        scalar.wait_ge(sems["sCI"], cis)
                    if w % 2 == 0:
                        scalar.activation(arena1[:], ps_big[:, 0:2048], AF.Copy)
                    else:
                        for tl in range(NTT):
                            scalar.activation(arena1[:, 512 * tl:512 * (tl + 1)],
                                              [ps_a, ps_b, ps_c, ps_d][tl][:, 0:512],
                                              AF.Copy)
                    inc(scalar.drain(), "sACT")
                SC(_)
                arena_evac[w] = cnt["sACT"]

                def _(gpsimd, w=w, par=par, ae=cnt["sACT"]):
                    gpsimd.wait_ge(sems["sACT"], ae)
                    for tl in range(NTT):
                        inc(gpsimd.dma_start(
                            cc_in[par][128 * tl:128 * (tl + 1),
                                       512 * w:512 * (w + 1)],
                            arena1[:, 512 * tl:512 * (tl + 1)]), "sCI", 16)
                GP(_)

            def _(gpsimd, par=par, cis=cnt["sCI"]):
                gpsimd.wait_ge(sems["sCI"], cis)
                if CC_FAKE:
                    inc(gpsimd.dma_start(cc_out[par], cc_in[par][0:T_M, :]),
                        "sCC", 16)
                else:
                    inc(gpsimd.collective_compute(
                        "ReduceScatter", ALU.add, replica_groups=CC_GROUPS,
                        ins=[cc_in[par].opt()], outs=[cc_out[par].opt()]), "sCC")
            GP(_)

            join()
            for tl in range(NTT2):
                def _(sync, tl=tl, ci=ci, par=par, ccn=cnt["sCC"],
                      dvew=cnt["sDVE"]):
                    sync.wait_ge(sems["sCC"], ccn)
                    if tl:
                        sync.wait_ge(sems["sDVE"], dvew)
                    inc(sync.dma_start(arena1[:],
                                       cc_out[par][128 * tl:128 * (tl + 1), :]),
                        "sHS", 16)
                    inc(sync.dma_start(
                        q_b[:, 0:2048],
                        hs2_d[ci * T_M + 128 * tl:ci * T_M + 128 * (tl + 1), :]),
                        "sHS", 16)
                SY(_)

                def _(vector, tl=tl, hsl=cnt["sHS"]):
                    vector.wait_ge(sems["sHS"], hsl)
                    vector.tensor_add(x2_tm[:, tl, :], arena1[:], q_b[:, 0:2048])
                    if RACE_SAFE:
                        vector.drain()
                    inc(vector.drain(), "sDVE")
                VE(_)

            for tl in range(NTT2):
                def _(scalar, tl=tl):
                    scalar.wait_ge(sems["sDVE"], cnt["sDVE"])
                    scalar.wait_ge(sems["sPE"], cnt["sPE"])
                    xa = x2_tm[:, tl, :]
                    bi = scalar.activation(ao_sb[:, 0:2048], xa, AF.Identity,
                                           accum_out=lncols[:, 0:1])
                    bi = scalar.activation(q_b[:, 0:2048], xa, AF.Square,
                                           accum_out=lncols[:, 1:2])
                    inc(bi, "sACT")
                SC(_)

                def _(vector, tl=tl):
                    vector.wait_ge(sems["sACT"], cnt["sACT"])
                    vector.tensor_scalar_mul(lncols[:, 2:3], lncols[:, 0:1],
                                             1.0 / D_MODEL)
                    vector.tensor_scalar_mul(lncols[:, 3:4], lncols[:, 1:2],
                                             1.0 / D_MODEL)
                    if RACE_SAFE:
                        vector.drain()
                    vector.tensor_mul(lncols[:, 4:5], lncols[:, 2:3], lncols[:, 2:3])
                    if RACE_SAFE:
                        vector.drain()
                    inc(vector.tensor_sub(lncols[:, 5:6], lncols[:, 3:4],
                                          lncols[:, 4:5]), "sDVE")
                VE(_)

                def _(scalar, tl=tl):
                    scalar.wait_ge(sems["sDVE"], cnt["sDVE"])
                    scalar.activation(lncols[:, 8:9], lncols[:, 5:6], AF.Ln,
                                      bias=lncols[:, 9:10])
                    scalar.drain()
                    inc(scalar.activation(lncols[:, 8:9], lncols[:, 8:9], AF.Exp,
                                          scale=0.5), "sACT")
                SC(_)

                def _(vector, tl=tl):
                    vector.wait_ge(sems["sACT"], cnt["sACT"])
                    vector.reciprocal(lncols[:, 6:7], lncols[:, 8:9])
                    if RACE_SAFE:
                        vector.drain()
                    inc(vector.scalar_tensor_tensor(lncols[:, 7:8], lncols[:, 2:3],
                                                    -1.0, lncols[:, 6:7], ALU.mult,
                                                    ALU.mult), "sDVE")
                VE(_)

                def _(scalar, tl=tl):
                    scalar.wait_ge(sems["sDVE"], cnt["sDVE"])
                    inc(scalar.activation(x2n[:, 0:2048], x2_tm[:, tl, :], AF.Identity,
                                          scale=lncols[:, 6:7], bias=lncols[:, 7:8]),
                        "sACT")
                SC(_)

                def _(vector, tl=tl):
                    vector.wait_ge(sems["sACT"], cnt["sACT"])
                    emit_bs_quant(vector, x2n[:, 0:2048].rearrange("p (a b) -> p a b", b=16),
                                  2048, colbuf[:, 1:2], colbuf[:, 4:5])
                    emit_xq(vector, x2n[:, 0:2048], 2048, stagebf[:])
                    inc(vector.drain(), "sDVE")
                VE(_)

                def _(tensor, tl=tl):
                    tensor.wait_ge(sems["sDVE"], cnt["sDVE"])
                    tensor.wait_ge(sems["sACT"], cnt["sACT"])
                    bi = None
                    for kd in range(KT):
                        bi = tensor.transpose(psb16[:, 128 * kd:128 * (kd + 1)],
                                              stagebf[:, 128 * kd:128 * (kd + 1)],
                                              identb_sb[:])
                    inc(bi, "sPE")
                PE(_)

                def _(scalar, tl=tl):
                    scalar.wait_ge(sems["sPE"], cnt["sPE"])
                    inc(scalar.activation(
                        xdq_t[:, :, 128 * tl:128 * (tl + 1)],
                        psb16[:, 0:2048].rearrange("p (a b) -> p a b", b=128),
                        AF.Copy), "sACT")
                SC(_)

            join()

            def fslot(i):
                if i < 4:
                    return ps_big[:, 512 * i:512 * (i + 1)]
                return [ps_a, ps_b, ps_c, ps_d][i - 4][:, :]

            for kd in range(KT):
                pq = wqpar[0]
                wqpar[0] ^= 1

                wqs = "sWS%d" % pq

                def _(sync, kd=kd, pq=pq, need=wqfree[pq], wqs=wqs):
                    if need:
                        sync.wait_ge(sems["sPE"], need)
                    inc(sync.dma_start(
                        wqslab[:, pq, :].rearrange("p (c a j) -> p c a j",
                                                   c=8, a=2),
                        agout5.rearrange("c a b p j -> b p c a j")[kd]),
                        wqs, 16)
                SY(_)
                wq_ld = cnt[wqs]

                def _(tensor, kd=kd, pq=pq, wq_ld=wq_ld, wqs=wqs):
                    tensor.wait_ge(sems[wqs], wq_ld)
                    bi = None
                    for tl in range(NTT2):
                        for ncol in range(4):
                            bi = tensor.matmul(
                                fslot(tl * 4 + ncol),
                                xdq_t[:, kd, 128 * tl:128 * (tl + 1)],
                                wqslab[:, pq, 512 * ncol:512 * (ncol + 1)],
                                start=(kd == 0), stop=(kd == KT - 1))
                    inc(bi, "sPE")
                PE(_)
                wqfree[pq] = cnt["sPE"]
            pe_ffn = cnt["sPE"]

            for tl in range(NTT2):
                for ncol in range(4):
                    def _(vector, tl=tl, ncol=ncol, pe_ffn=pe_ffn):
                        vector.wait_ge(sems["sPE"], pe_ffn)
                        vector.wait_ge(sems["sST"], cnt["sST"])
                        vector.scalar_tensor_tensor(
                            q_a[:, 0:512], fslot(tl * 4 + ncol), colbuf[:, 0:1],
                            ffnb_bc[:, 512 * ncol:512 * (ncol + 1)],
                            ALU.mult, ALU.add)
                        if RACE_SAFE:
                            vector.drain()
                        vector.tensor_add(
                            q_a[:, 512:1024], q_a[:, 0:512],
                            x2_tm[:, tl, 512 * ncol:512 * (ncol + 1)])
                        if RACE_SAFE:
                            vector.drain()
                        inc(vector.drain(), "sDVE")
                    VE(_)

                    def _(gpsimd, tl=tl, ncol=ncol, ci=ci):
                        gpsimd.wait_ge(sems["sDVE"], cnt["sDVE"])
                        inc(gpsimd.dma_start(
                            out_d[ci * T_M + 128 * tl:ci * T_M + 128 * (tl + 1),
                                  512 * ncol:512 * (ncol + 1)],
                            q_a[:, 512:1024]), "sST", 16)
                    GP(_)

        def _(sync):
            sync.wait_ge(sems["sST"], cnt["sST"])
        SY(_)

    st.close()
    return nc


_NC_CACHE = None


def kernel(L0_attn_hidden_states, L0_attn_in_proj_w, L0_attn_dt_in_proj_w,
           L0_attn_dt_proj_w, L0_attn_dt_proj_b, L0_attn_conv_w, L0_attn_conv_b,
           L0_attn_A_log, L0_attn_D, L0_attn_out_proj_w, L0_ffn_weight,
           L0_ffn_bias, L0_ffn_x_tensor_scale):
    global _NC_CACHE
    hs = np.ascontiguousarray(np.asarray(L0_attn_hidden_states, np.float32))
    ipw = np.asarray(L0_attn_in_proj_w, np.float32)
    dtin_t = _tile_lhsT(np.asarray(L0_attn_dt_in_proj_w, np.float32))[0]
    dtpw = np.asarray(L0_attn_dt_proj_w, np.float32)
    dtbv = np.asarray(L0_attn_dt_proj_b, np.float32)
    convw = np.asarray(L0_attn_conv_w, np.float32)
    convbv = np.asarray(L0_attn_conv_b, np.float32)
    alog = np.asarray(L0_attn_A_log, np.float32)
    dvec = np.asarray(L0_attn_D, np.float32)
    wo = np.asarray(L0_attn_out_proj_w, np.float32)
    wffn = np.ascontiguousarray(
        np.asarray(L0_ffn_weight, np.float32).reshape(KT, 128, D_MODEL))
    ffnb = np.asarray(L0_ffn_bias, np.float32).reshape(1, D_MODEL)
    xts = np.asarray(L0_ffn_x_tensor_scale, np.float32).reshape(1, 1)

    base = dict(
        dtin_t=dtin_t, ffnb=ffnb, xts=xts,
        bsel=_b_sel(), csel=_c_sel(),
        ones_row=np.ones((128, T_C), ml_dtypes.bfloat16),
        identb=np.eye(128, dtype=ml_dtypes.bfloat16),
        identf=np.eye(128, dtype=np.float32),
    )
    in_maps = []
    for c in range(8):
        b, h = c // 2, c % 2
        m_wffn = wffn[2 * c:2 * c + 2]
        ch = slice(D_LOC * h, D_LOC * (h + 1))
        xb = slice(512 * h, 512 * (h + 1))
        m = dict(base)
        wt_loc = np.concatenate([
            ipw[0:D_INNER][ch],
            ipw[D_INNER:D_INNER + 1024][xb],
            ipw[D_INNER + 1024:D_INNER + 2048][xb],
            ipw[D_INNER + 2048:][ch],
        ], axis=0)
        m["wt_all"] = _tile_lhsT(wt_loc)
        dtp = _tile_lhsT(dtpw[ch])
        m["dtp_t"] = np.ascontiguousarray(
            dtp.transpose(1, 0, 2).reshape(128, G * 128))
        m["dtb"] = dtbv[ch].reshape(1, D_LOC)
        m["convsel"] = _conv_sel(convw[ch]).reshape(G * 4, 128, 128)
        m["convb"] = convbv[ch].reshape(1, D_LOC)
        m["alog_t"] = np.ascontiguousarray(
            alog[ch].reshape(G, 128, D_STATE).transpose(1, 0, 2)
            .reshape(128, G * D_STATE))
        m["dvec_t"] = np.ascontiguousarray(dvec[ch].reshape(G, 128).T)
        m["wo_t"] = _tile_lhsT(np.ascontiguousarray(wo[:, ch]))
        m["wffn"] = np.ascontiguousarray(m_wffn)
        m["hs"] = hs[b]
        m["hs2"] = np.ascontiguousarray(
            hs[b].reshape(N_CHUNK, 2, T_M, D_MODEL)[:, h]
            .reshape(N_CHUNK * T_M, D_MODEL))
        in_maps.append(m)

    if _NC_CACHE is None:
        _NC_CACHE = build_program()
    res = run_bass_kernel_spmd(_NC_CACHE, in_maps, core_ids=list(range(8)))
    out = np.zeros((4, L, D_MODEL), np.float32)
    for c in range(8):
        b, h = c // 2, c % 2
        r = np.asarray(res.results[c]["out_tm"]).reshape(N_CHUNK, T_M, D_MODEL)
        out[b].reshape(N_CHUNK, 2, T_M, D_MODEL)[:, h] = r
    return out
